# version 15
# baseline (speedup 1.0000x reference)
import numpy as np
import ml_dtypes
from contextlib import ExitStack

import concourse.bass as bass
import concourse.mybir as mybir
from concourse.bass_utils import run_bass_kernel_spmd

F32 = mybir.dt.float32
BF16 = mybir.dt.bfloat16
AF = mybir.ActivationFunctionType
OP = mybir.AluOpType
AX = mybir.AxisListType

L = 2048
DM = 2048
DI = 4096
DS = 16
DCV = 4
DR = 128
FH = 4096
T = 512
NCH = L // T
NTT = T // 128
NMK = DM // 128
NDC = DI // 128
NSLOT = 2
EPS = 1e-5
CB = 4096

ENGS = ("tensor", "vector", "scalar", "gpsimd", "sync")

S1SUM, S1SSQ, CMU, CNVAR, CSTD, CRSTD, CNMRS, CDLG, CMASK = \
    0, 16, 20, 24, 28, 32, 36, 40, 44
S2SUM, S2SSQ, C2MU, C2NVAR, C2STD, C2RSTD, C2NMRS = 48, 64, 68, 72, 76, 80, 84


class KB:

    CUMUL = ("qw", "qo", "qg", "qx", "qc")

    def __init__(self, nc, es):
        self.nc = nc
        self.es = es
        self.C = {k: 0 for k in ("qw", "qo", "qg", "qx", "qc", "qd",
                                 "w0", "w1", "w2",
                                 "T", "V", "S", "P", "Y")}
        self.OWN = {"tensor": "T", "vector": "V", "scalar": "S",
                    "gpsimd": "P", "sync": "Y"}
        self.QUEUE = {"sync": "qw", "scalar": "qo", "gpsimd": "qg"}
        self.ENG_DMA = {"sync": ["qw", "qx"] + [f"w{i}" for i in range(NSLOT)],
                        "scalar": ["qo"] + [f"w{i}" for i in range(NSLOT)],
                        "gpsimd": ["qg"]}
        self.waited = {e: {} for e in ENGS}
        self.sem = {}
        self.q = None
        self.join_gate = {e: None for e in ENGS}
        for k in self.C:
            self.sem[k] = es.enter_context(nc.semaphore("sem_" + k))
        self.slot_rel = [None] * NSLOT
        self.widx = 0
        self.WSTR = None
        self.cast_done = {}

    def _waits(self, engname, waits):
        wl = []
        for s, v in waits:
            if v is None:
                continue
            if isinstance(v, tuple):
                raise TypeError(f"bad gate {s} {v}")
            if v <= 0:
                continue
            assert v <= self.C[s], f"wait on future {s}={v} > {self.C[s]}"
            if s in self.CUMUL:
                v = self.C[s]
            if self.OWN[engname] == s:
                continue
            if self.waited[engname].get(s, -1) >= v:
                continue
            self.waited[engname][s] = v
            wl.append((s, v))
        return wl

    def op(self, engname, fn, waits=(), inc=None, inc_n=1, drain=False):
        wl = self._waits(engname, waits)
        gate = None
        if inc is not None:
            self.C[inc] += inc_n
            gate = (inc, self.C[inc])
        sems = self.sem

        def run(eng):
            if drain:
                eng.drain()
            for s, v in wl:
                eng.wait_ge(sems[s], v)
            ins = fn(eng)
            if inc is not None:
                if ins is None:
                    ins = eng.nop()
                ins.then_inc(sems[inc], inc_n)
        self.q[engname].append(run)
        return gate

    def dma(self, engname, out_ap, in_ap, waits=(), allow_slow=False,
            qsem=None):
        qs = qsem or self.QUEUE[engname]

        def fn(eng):
            if allow_slow:
                return eng.dma_start(out_ap, in_ap,
                                     allow_slow_non_contiguous=True)
            return eng.dma_start(out_ap, in_ap)
        return self.op(engname, fn, waits=waits, inc=qs, inc_n=16)

    def wload(self, src2d, r0, c0, waits=()):
        sl = self.widx % NSLOT
        eng = "sync" if (self.widx % 2 == 0) else "scalar"
        w = list(waits)
        if self.slot_rel[sl] is not None:
            w.append(self.slot_rel[sl])
        src = src2d[r0:r0 + 1024, c0:c0 + 512].rearrange(
            "(m p) f -> p m f", p=128)
        dst = self.WSTR[:, sl * 4096:(sl + 1) * 4096].rearrange(
            "p (m f) -> p m f", m=8)
        g = self.dma(eng, dst, src, waits=w, qsem=f"w{sl}")
        self.widx += 1
        return sl, g

    def emit_cast(self, src, dst, name, qsem="qc", pieces=2):
        rows = src.shape[0]
        assert rows % pieces == 0
        rp = rows // pieces
        g = None
        for i in range(pieces):
            g = self.dma("gpsimd", dst.ap()[i * rp:(i + 1) * rp, :],
                         src.ap()[i * rp:(i + 1) * rp, :], qsem=qsem)
        self.cast_done[name] = g

    def begin_phase(self):
        self.q = {e: [] for e in ENGS}
        gates = [g for g in self.join_gate.values() if g is not None]
        for e in ENGS:
            wl = self._waits(e, gates)
            if wl:
                sems = self.sem

                def mk(wl):
                    def run(eng):
                        for s, v in wl:
                            eng.wait_ge(sems[s], v)
                    return run
                self.q[e].append(mk(wl))

    def end_phase(self, name):
        for e in ENGS:
            waits = []
            for qs in self.ENG_DMA.get(e, ()):
                if self.C[qs] > 0:
                    waits.append((qs, self.C[qs]))
            self.join_gate[e] = self.op(e, lambda eng: eng.drain(),
                                        waits=waits, inc=self.OWN[e])
        qq = self.q
        with self.nc.Block(name) as blk:
            @blk.tensor
            def _(eng):
                for f in qq["tensor"]:
                    f(eng)

            @blk.vector
            def _(eng):
                for f in qq["vector"]:
                    f(eng)

            @blk.scalar
            def _(eng):
                for f in qq["scalar"]:
                    f(eng)

            @blk.gpsimd
            def _(eng):
                for f in qq["gpsimd"]:
                    f(eng)

            @blk.sync
            def _(eng):
                for f in qq["sync"]:
                    f(eng)
        self.q = None


class NS:
    pass


def declare_dram(nc):
    d = NS()
    d.x = nc.dram_tensor("x", [L, DM], F32, kind="ExternalInput")
    d.xT = nc.dram_tensor("xT", [DM, L], F32, kind="ExternalInput")
    d.winT = nc.dram_tensor("winT", [DM, 2 * DI], F32, kind="ExternalInput")
    d.convw = nc.dram_tensor("convw", [DI, DCV], F32, kind="ExternalInput")
    d.convb = nc.dram_tensor("convb", [DI], F32, kind="ExternalInput")
    d.wxT = nc.dram_tensor("wxT", [DI, DR + 2 * DS], F32, kind="ExternalInput")
    d.wdtT = nc.dram_tensor("wdtT", [DR, DI], F32, kind="ExternalInput")
    d.bdt = nc.dram_tensor("bdt", [DI], F32, kind="ExternalInput")
    d.alog = nc.dram_tensor("alog", [DI, DS], F32, kind="ExternalInput")
    d.dd = nc.dram_tensor("dd", [DI], F32, kind="ExternalInput")
    d.woutT = nc.dram_tensor("woutT", [DI, DM], F32, kind="ExternalInput")
    d.wg = nc.dram_tensor("wg", [2, DM], F32, kind="ExternalInput")
    d.w0aT = nc.dram_tensor("w0aT", [DM, FH], F32, kind="ExternalInput")
    d.b0a = nc.dram_tensor("b0a", [FH], F32, kind="ExternalInput")
    d.w0bT = nc.dram_tensor("w0bT", [FH, DM], F32, kind="ExternalInput")
    d.b0b = nc.dram_tensor("b0b", [DM], F32, kind="ExternalInput")
    d.w1aT = nc.dram_tensor("w1aT", [DM, FH], F32, kind="ExternalInput")
    d.b1a = nc.dram_tensor("b1a", [FH], F32, kind="ExternalInput")
    d.w1bT = nc.dram_tensor("w1bT", [FH, DM], F32, kind="ExternalInput")
    d.b1b = nc.dram_tensor("b1b", [DM], F32, kind="ExternalInput")
    d.cid = nc.dram_tensor("cid", [128, 128], BF16, kind="ExternalInput")
    d.cones = nc.dram_tensor("cones", [1, 128], F32, kind="ExternalInput")
    d.conesb = nc.dram_tensor("conesb", [1, 128], BF16, kind="ExternalInput")
    d.esel = nc.dram_tensor("esel", [32, 32 * 128], BF16, kind="ExternalInput")
    d.out = nc.dram_tensor("out", [L, DM], F32, kind="ExternalOutput")
    d.winb = nc.dram_tensor("winb_s", [DM, 2 * DI], BF16)
    d.woutb = nc.dram_tensor("woutb_s", [DI, DM], BF16)
    d.w0ab = nc.dram_tensor("w0ab_s", [DM, FH], BF16)
    d.w0bb = nc.dram_tensor("w0bb_s", [FH, DM], BF16)
    d.w1ab = nc.dram_tensor("w1ab_s", [DM, FH], BF16)
    d.w1bb = nc.dram_tensor("w1bb_s", [FH, DM], BF16)
    return d


def alloc_resident(nc, es):
    r = NS()
    sb = lambda n, sh, dt: es.enter_context(nc.sbuf_tensor(n, sh, dt))
    r.WXT = sb("WXT", [128, NDC * 160], BF16)
    r.WDTT = sb("WDTT", [128, DI], BF16)
    r.ANEG = sb("ANEG", [128, NDC * DS], F32)
    r.DCOL = sb("DCOL", [128, NDC], F32)
    r.CONVB = sb("CONVB", [128, NDC], F32)
    r.BDTC = sb("BDTC", [128, NDC], F32)
    r.B0AC = sb("B0AC", [128, NDC], F32)
    r.B1AC = sb("B1AC", [128, NDC], F32)
    r.CONVW = sb("CONVW", [128, NDC * DCV], F32)
    r.GDIFF = sb("GDIFF", [128, DM], F32)
    r.B0BBC = sb("B0BBC", [128, DM], BF16)
    r.B1BBC = sb("B1BBC", [128, DM], BF16)
    r.IDT = sb("IDT", [128, 128], BF16)
    r.ONESF = sb("ONESF", [1, 128], F32)
    r.ONESB = sb("ONESB", [1, 128], BF16)
    r.ESEL = sb("ESEL", [32, 32 * 128], BF16)
    r.CONSTS = sb("CONSTS", [128, 4], F32)
    r.HCARRY = sb("HCARRY", [128, NDC * DS], F32)
    r.TAILS = sb("TAILS", [128, NDC * 3], F32)
    r.STATS = sb("STATS", [128, 96], F32)
    r.JUNK = sb("JUNK", [128, DM], BF16)
    r.XNTMB = sb("XNTMB", [128, NTT * DM], BF16)
    r.WSTR = sb("WSTR", [128, NSLOT * 4096], BF16)
    return r


def emit_prologue(nc, kb, d, r, XBF0):
    es = ExitStack()
    sb = lambda n, sh, dt: es.enter_context(nc.sbuf_tensor(n, sh, dt))
    PROW = sb("PROW", [1, 2 * DM], F32)
    PPA = es.enter_context(nc.psum_tensor("PPA", [128, 512], F32))

    kb.begin_phase()

    for mk in range(NMK):
        kb.dma("gpsimd", XBF0[:, mk * T:(mk + 1) * T],
               d.xT.ap()[mk * 128:(mk + 1) * 128, 0:T], qsem="qx")
    for dc in range(NDC):
        kb.dma("gpsimd", r.WXT[:, dc * 160:(dc + 1) * 160],
               d.wxT.ap()[dc * 128:(dc + 1) * 128, :], qsem="qg")
    kb.dma("gpsimd", r.WDTT[:], d.wdtT.ap(), qsem="qg")

    kb.emit_cast(d.winT, d.winb, "win", qsem="qd", pieces=4)
    kb.emit_cast(d.woutT, d.woutb, "wout")
    kb.emit_cast(d.w0aT, d.w0ab, "w0a")
    kb.emit_cast(d.w0bT, d.w0bb, "w0b")
    kb.emit_cast(d.w1aT, d.w1ab, "w1a")
    kb.emit_cast(d.w1bT, d.w1bb, "w1b")

    g_alog = None
    for dc in range(NDC):
        g_alog = kb.dma("sync", r.ANEG[:, dc * DS:(dc + 1) * DS],
                        d.alog.ap()[dc * 128:(dc + 1) * 128, :])
    for vec, dst in ((d.dd, r.DCOL), (d.convb, r.CONVB), (d.bdt, r.BDTC),
                     (d.b0a, r.B0AC), (d.b1a, r.B1AC)):
        for dc in range(NDC):
            kb.dma("sync", dst[:, dc:dc + 1],
                   vec.ap()[dc * 128:(dc + 1) * 128].unsqueeze(1))
    for dc in range(NDC):
        kb.dma("sync", r.CONVW[:, dc * DCV:(dc + 1) * DCV],
               d.convw.ap()[dc * 128:(dc + 1) * 128, :])
    kb.dma("sync", r.IDT[:], d.cid.ap())
    g_of = kb.dma("sync", r.ONESF[:], d.cones.ap())
    kb.dma("sync", r.ONESB[:], d.conesb.ap())
    kb.dma("sync", r.ESEL[:], d.esel.ap())
    g_wg = kb.dma("sync", PROW[:].rearrange("p (r m) -> p r m", r=2),
                  d.wg.ap().unsqueeze(0))

    gA = kb.op("scalar", lambda e: e.activation(r.ANEG[:], r.ANEG[:], AF.Exp),
               waits=[g_alog], inc="S")
    kb.op("scalar", lambda e: e.mul(r.ANEG[:], r.ANEG[:], -1.0),
          waits=[gA], inc="S")

    kb.op("vector", lambda e: e.memset(r.CONSTS[:, 0:1], EPS), inc="V")
    kb.op("vector", lambda e: e.memset(r.CONSTS[:, 1:2], -1.0), inc="V")
    kb.op("vector", lambda e: e.memset(r.CONSTS[:, 2:3], 1.0), inc="V")
    kb.op("vector", lambda e: e.memset(r.HCARRY[:], 0.0), inc="V")
    kb.op("vector", lambda e: e.memset(r.TAILS[:], 0.0), inc="V")

    g_gs = kb.op("vector", lambda e: e.tensor_sub(
        PROW[0:1, 0:DM], PROW[0:1, 0:DM], PROW[0:1, DM:2 * DM]),
        waits=[g_wg], inc="V")

    ev = None
    gm = None
    for m in range(4):
        w = [g_of, g_gs]
        if ev is not None:
            w.append(ev)
        gm = kb.op("tensor", (lambda m: lambda e: e.matmul(
            PPA[:], r.ONESF[0:1, :], PROW[0:1, m * 512:(m + 1) * 512],
            start=True, stop=True))(m), waits=w, inc="T")
        ev = kb.op("vector", (lambda m: lambda e: e.tensor_copy(
            r.GDIFF[:, m * 512:(m + 1) * 512], PPA[:]))(m),
            waits=[gm], inc="V")
    g_b0b = kb.dma("sync", PROW[0:1, 0:DM], d.b0b.ap().unsqueeze(0),
                   waits=[gm])
    g_b1b = kb.dma("sync", PROW[0:1, DM:2 * DM], d.b1b.ap().unsqueeze(0),
                   waits=[g_gs])
    for row_ap, dst, gsrc in ((PROW[0:1, 0:DM], r.B0BBC, g_b0b),
                              (PROW[0:1, DM:2 * DM], r.B1BBC, g_b1b)):
        for m in range(4):
            w = [g_of, gsrc, ev]
            gm = kb.op("tensor", (lambda ra, m: lambda e: e.matmul(
                PPA[:], r.ONESF[0:1, :], ra[:, m * 512:(m + 1) * 512],
                start=True, stop=True))(row_ap, m), waits=w, inc="T")
            ev = kb.op("vector", (lambda dst, m: lambda e: e.tensor_copy(
                dst[:, m * 512:(m + 1) * 512], PPA[:]))(dst, m),
                waits=[gm], inc="V")

    kb.end_phase("prologue")
    es.close()


def emit_m1(nc, kb, d, r, m, c, XBF0=None):
    es = ExitStack()
    sb = lambda n, sh, dt: es.enter_context(nc.sbuf_tensor(n, sh, dt))
    XBF = XBF0 if XBF0 is not None else sb(f"XBF{c}", [128, NMK * T], BF16)
    XSW = sb(f"XSW{c}", [128, 2 * 520], F32)
    ACC = sb(f"ACC{c}", [128, 2 * T], F32)
    PG = [es.enter_context(nc.psum_tensor(f"PG{i}_{c}", [128, 512], F32))
          for i in range(4)]
    PXD = es.enter_context(nc.psum_tensor(f"PXD{c}", [128, 512], F32))
    PXBC = es.enter_context(nc.psum_tensor(f"PXBC{c}", [128, 512], F32))
    t0 = c * T

    kb.begin_phase()

    if XBF0 is not None:
        xbf_gate = [("qx", 16 * (mk + 1)) for mk in range(NMK)]
    else:
        xbf_gate = []
        for mk in range(NMK):
            xbf_gate.append(kb.dma(
                "gpsimd", XBF[:, mk * T:(mk + 1) * T],
                d.xT.ap()[mk * 128:(mk + 1) * 128, t0:t0 + T], qsem="qx"))

    xs_ev = [None] * 4
    z_ev = [None] * 4
    u_gate = [None] * NDC
    silu_gate = [None, None]
    xdbl_pending = []
    xdbl_last = [None]

    def emit_xdbl(fc):
        kb.op("tensor", (lambda fc: lambda e: e.matmul(
            PXD[:], r.WXT[:, fc * 160:fc * 160 + 128], m.U[:, fc * T:(fc + 1) * T],
            start=(fc == 0), stop=(fc == NDC - 1)))(fc),
            waits=[u_gate[fc]], inc="T")
        xdbl_last[0] = kb.op("tensor", (lambda fc: lambda e: e.matmul(
            PXBC[0:32, :], r.WXT[:, fc * 160 + 128:fc * 160 + 160],
            m.U[:, fc * T:(fc + 1) * T],
            start=(fc == 0), stop=(fc == NDC - 1)))(fc), waits=[], inc="T")

    for g in range(16):
        is_z = g >= 8
        gm = None
        for half in range(2):
            wc = [kb.cast_done["win"]]
            sl, gw = kb.wload(d.winb.ap(), half * 1024, g * 512, waits=wc)
            for jj in range(8):
                mk = half * 8 + jj
                for fcw in range(4):
                    w = [gw, xbf_gate[mk]]
                    if mk == 0:
                        eg = (z_ev if is_z else xs_ev)[fcw]
                        if eg is not None:
                            w.append(eg)
                        if is_z and g == 8 and xs_ev[fcw] is not None:
                            w.append(xs_ev[fcw])
                    gm = kb.op("tensor", (lambda sl, jj, fcw, mk: lambda e: e.matmul(
                        PG[fcw][:],
                        r.WSTR[:, sl * 4096 + jj * 512 + fcw * 128:
                               sl * 4096 + jj * 512 + (fcw + 1) * 128],
                        XBF[:, mk * T:(mk + 1) * T],
                        start=(mk == 0), stop=(mk == NMK - 1)))(sl, jj, fcw, mk),
                        waits=w, inc="T")
            kb.slot_rel[sl] = gm
        grp_done = gm

        if not is_z:
            for fcw in range(4):
                fc = g * 4 + fcw
                bi = fc % 2
                w = [grp_done]
                if silu_gate[bi] is not None:
                    w.append(silu_gate[bi])
                kb.op("scalar", (lambda bi, fc: lambda e: e.copy(
                    XSW[:, bi * 520:bi * 520 + 3],
                    r.TAILS[:, fc * 3:fc * 3 + 3]))(bi, fc), waits=w, inc="S")
                kb.op("scalar", (lambda bi, fcw: lambda e: e.copy(
                    XSW[:, bi * 520 + 3:bi * 520 + 3 + T], PG[fcw][:]))(bi, fcw),
                    waits=[], inc="S")
                gt3 = kb.op("scalar", (lambda fc, fcw: lambda e: e.copy(
                    r.TAILS[:, fc * 3:fc * 3 + 3], PG[fcw][:, T - 3:T]))(fc, fcw),
                    waits=[], inc="S")
                xs_ev[fcw] = gt3
                gcv = kb.op("vector", (lambda bi, fc: lambda e: e.tensor_scalar(
                    ACC[:, bi * T:(bi + 1) * T], XSW[:, bi * 520:bi * 520 + T],
                    r.CONVW[:, fc * 4:fc * 4 + 1], r.CONVB[:, fc:fc + 1],
                    OP.mult, OP.add))(bi, fc), waits=[gt3], inc="V")
                for k in range(1, DCV):
                    gcv = kb.op("vector", (lambda bi, fc, k: lambda e:
                                e.scalar_tensor_tensor(
                                    ACC[:, bi * T:(bi + 1) * T],
                                    XSW[:, bi * 520 + k:bi * 520 + k + T],
                                    r.CONVW[:, fc * 4 + k:fc * 4 + k + 1],
                                    ACC[:, bi * T:(bi + 1) * T],
                                    OP.mult, OP.add))(bi, fc, k),
                                waits=[], inc="V")
                gu = kb.op("scalar", (lambda bi, fc: lambda e: e.activation(
                    m.U[:, fc * T:(fc + 1) * T], ACC[:, bi * T:(bi + 1) * T],
                    AF.Silu))(bi, fc), waits=[gcv], inc="S")
                silu_gate[bi] = gu
                u_gate[fc] = gu
                xdbl_pending.append(fc)
            if g >= 1:
                for fc in xdbl_pending[:4]:
                    emit_xdbl(fc)
                xdbl_pending[:4] = []
        else:
            if g == 8:
                for fc in xdbl_pending:
                    emit_xdbl(fc)
                xdbl_pending.clear()
            for fcw in range(4):
                zc = (g - 8) * 4 + fcw
                z_ev[fcw] = kb.op("scalar", (lambda zc, fcw: lambda e:
                                  e.activation(m.SILUZ[:, zc * T:(zc + 1) * T],
                                               PG[fcw][:], AF.Silu))(zc, fcw),
                                  waits=[grp_done], inc="S")

    kb.op("scalar", lambda e: e.copy(m.DTRB[:], PXD[:]),
          waits=[xdbl_last[0]], inc="S")
    kb.op("scalar", lambda e: e.copy(m.BCR[:], PXBC[0:32, :]), waits=[], inc="S")

    kb.end_phase(f"m1_{c}")
    es.close()


def emit_m2(nc, kb, d, r, m, c):
    es = ExitStack()
    sb = lambda n, sh, dt: es.enter_context(nc.sbuf_tensor(n, sh, dt))
    BBC = sb(f"BBC{c}", [128, DS * T], BF16)
    CBC = sb(f"CBC{c}", [128, DS * T], BF16)
    YMM = sb(f"YMM{c}", [128, T], BF16)
    YACC = sb(f"YACC{c}", [128, T], BF16)
    DTT = sb(f"DTT{c}", [128, T], BF16)
    DUU2 = sb(f"DUU2{c}", [128, 2 * T], BF16)
    DAA = sb(f"DAA{c}", [128, 4 * T], BF16)
    DBU = sb(f"DBU{c}", [128, 4 * T], BF16)
    HSSW = sb(f"HSSW{c}", [128, 8 * T], BF16)
    XRES = sb(f"XRES{c}", [128, 512], F32)
    PB = [es.enter_context(nc.psum_tensor(f"PBB{i}_{c}", [128, 512], F32))
          for i in range(2)]
    PW = [es.enter_context(nc.psum_tensor(f"PWW{i}_{c}", [128, 512], F32))
          for i in range(2)]
    t0 = c * T

    kb.begin_phase()

    bc_ev = [None, None]
    bbc_gate = [None] * DS
    cbc_gate = [None] * DS
    for i in range(2 * DS):
        isC = i >= DS
        s = i % DS
        pb = PB[i % 2]
        w = []
        if bc_ev[i % 2] is not None:
            w.append(bc_ev[i % 2])
        gm = kb.op("tensor", (lambda pb, i: lambda e: e.matmul(
            pb[:], r.ESEL[:, i * 128:(i + 1) * 128], m.BCR[:],
            start=True, stop=True))(pb, i), waits=w, inc="T")
        dst = CBC if isC else BBC
        ge = kb.op("scalar", (lambda pb, dst, s: lambda e: e.copy(
            dst[:, s * T:(s + 1) * T], pb[:]))(pb, dst, s),
            waits=[gm], inc="S")
        bc_ev[i % 2] = ge
        (cbc_gate if isC else bbc_gate)[s] = ge

    pw_ev = [None, None]
    da_slot = [None] * 4
    dbu_rd = [None, None]
    g_gate = [None] * NDC
    sp_last = None
    last_carry = None
    st = NS()
    st.gsp = {}
    st.gsp5_prev = None

    def emit_prep_a(dc):
        pw = PW[dc % 2]
        w = [pw_ev[dc % 2]] if pw_ev[dc % 2] is not None else []
        gm = kb.op("tensor", (lambda pw, dc: lambda e: e.matmul(
            pw[:], r.WDTT[:, dc * 128:(dc + 1) * 128], m.DTRB[:],
            start=True, stop=True))(pw, dc), waits=w, inc="T")
        w2 = [gm]
        if st.gsp5_prev is not None:
            w2.append(st.gsp5_prev)
        kb.op("scalar", (lambda pw, dc: lambda e: e.activation(
            r.JUNK[:, 0:T], pw[:], AF.Abs,
            bias=r.BDTC[:, dc:dc + 1]))(pw, dc), waits=w2, inc="S")
        kb.op("scalar", lambda e: e.activation(
            r.JUNK[:, 0:T], r.JUNK[:, 0:T], AF.Exp, scale=-1.0), inc="S")
        kb.op("scalar", lambda e: e.activation(
            r.JUNK[:, 0:T], r.JUNK[:, 0:T], AF.Ln, bias=1.0), inc="S")
        gsp = kb.op("scalar", (lambda pw, dc: lambda e: e.activation(
            r.JUNK[:, T:2 * T], pw[:], AF.Relu,
            bias=r.BDTC[:, dc:dc + 1]))(pw, dc), waits=[], inc="S")
        pw_ev[dc % 2] = gsp
        st.gsp[dc] = gsp

    def emit_prep_b(dc):
        gsp5 = kb.op("vector", lambda e: e.tensor_add(
            DTT[:], r.JUNK[:, 0:T], r.JUNK[:, T:2 * T]),
            waits=[st.gsp[dc]], inc="V")
        st.gsp5_prev = gsp5
        w2 = [gsp5] + [g for g in dbu_rd if g is not None]
        gdu = kb.op("vector", (lambda dc: lambda e: e.tensor_mul(
            DUU2[:, 0:T], DTT[:],
            m.U[:, dc * T:(dc + 1) * T]))(dc), waits=w2, inc="V")
        gcp = kb.op("scalar", lambda e: e.copy(
            DUU2[:, T:2 * T], DUU2[:, 0:T]), waits=[gdu], inc="S")
        return gsp5, gcp

    X = r.XNTMB
    emit_prep_a(0)
    nxt = emit_prep_b(0)
    for dc in range(NDC):
        gsp5, gcp = nxt
        sp_last = st.gsp[dc]
        carry_h0 = None
        for q in range(8):
            wsl = q % 2
            w = [gcp, bbc_gate[2 * q + 1]]
            if dbu_rd[wsl] is not None:
                w.append(dbu_rd[wsl])
            gdb = kb.op("vector", (lambda wsl, q: lambda e: e.tensor_mul(
                DBU[:, wsl * 2 * T:(wsl + 1) * 2 * T], DUU2[:],
                BBC[:, 2 * q * T:(2 * q + 2) * T]))(wsl, q),
                waits=w, inc="V")
            for i2 in range(2):
                s = 2 * q + i2
                j = s % 4
                sl8 = s % 8
                h = s // 8
                col = dc * DS + s
                w = [gsp5]
                if da_slot[j] is not None:
                    w.append(da_slot[j])
                gda = kb.op("scalar", (lambda j, col: lambda e: e.activation(
                    DAA[:, j * T:(j + 1) * T],
                    DTT[:], AF.Exp,
                    scale=r.ANEG[:, col:col + 1]))(j, col), waits=w, inc="S")
                if s == DS - 1 and dc + 1 < NDC:
                    emit_prep_a(dc + 1)
                w = [gda, gdb]
                if h == 0 and last_carry is not None:
                    w.append(last_carry)
                if h == 1 and carry_h0 is not None:
                    w.append(carry_h0)
                gsc = kb.op("vector", (lambda sl8, j, wsl, i2, col: lambda e:
                            e.tensor_tensor_scan(
                                HSSW[:, sl8 * T:(sl8 + 1) * T],
                                DAA[:, j * T:(j + 1) * T],
                                DBU[:, (wsl * 2 + i2) * T:
                                    (wsl * 2 + i2 + 1) * T],
                                r.HCARRY[:, col:col + 1],
                                OP.mult, OP.add))(sl8, j, wsl, i2, col),
                            waits=w, inc="V")
                da_slot[j] = gsc
                if i2 == 1:
                    dbu_rd[wsl] = gsc
                if sl8 == 7:
                    c0 = dc * DS + h * 8
                    gcr = kb.op("scalar", (lambda c0: lambda e: e.copy(
                        r.HCARRY[:, c0:c0 + 8],
                        HSSW[:].rearrange("p (s t) -> p s t", s=8)
                        [:, :, T - 1:T].rearrange("p s o -> p (s o)")))(c0),
                        waits=[gsc], inc="S")
                    if h == 0:
                        carry_h0 = gcr
                    else:
                        last_carry = gcr
                    gwm = kb.op("vector", (lambda h: lambda e: e.tensor_mul(
                        X[:, h * 4096:(h + 1) * 4096], HSSW[:],
                        CBC[:, h * 8 * T:(h + 1) * 8 * T]))(h),
                        waits=[gsc, cbc_gate[h * 8 + 7]], inc="V")
                    if h == 0:
                        kb.op("vector", lambda e: e.tensor_add(
                            X[:, 4096:6144], X[:, 0:2048], X[:, 2048:4096]),
                            waits=[gwm], inc="V")
                        kb.op("vector", lambda e: e.tensor_add(
                            X[:, 6144:7168], X[:, 4096:5120], X[:, 5120:6144]),
                            inc="V")
                        gy0 = kb.op("vector", lambda e: e.tensor_add(
                            YACC[:], X[:, 6144:6656], X[:, 6656:7168]),
                            inc="V")
                    else:
                        if dc + 1 < NDC:
                            nxt = emit_prep_b(dc + 1)
                        kb.op("vector", lambda e: e.tensor_add(
                            X[:, 0:2048], X[:, 4096:6144], X[:, 6144:8192]),
                            waits=[gwm], inc="V")
                        kb.op("vector", lambda e: e.tensor_add(
                            X[:, 2048:3072], X[:, 0:1024], X[:, 1024:2048]),
                            inc="V")
                        gy1 = kb.op("vector", lambda e: e.tensor_add(
                            YMM[:], X[:, 2048:2560], X[:, 2560:3072]),
                            inc="V")
        gyd = kb.op("vector", (lambda dc: lambda e: e.scalar_tensor_tensor(
            YACC[:], m.U[:, dc * T:(dc + 1) * T],
            r.DCOL[:, dc:dc + 1], YACC[:],
            OP.mult, OP.add))(dc), waits=[gy0], inc="V")
        gfin = kb.op("vector", lambda e: e.tensor_add(
            HSSW[:, 0:T], YACC[:], YMM[:]),
            waits=[gyd, gy1, last_carry], inc="V")
        g_gate[dc] = kb.op("vector", (lambda dc: lambda e: e.tensor_mul(
            m.SILUZ[:, dc * T:(dc + 1) * T], m.SILUZ[:, dc * T:(dc + 1) * T],
            HSSW[:, 0:T]))(dc), waits=[gfin], inc="V")

    PO = [PB[0], PB[1], PW[0], PW[1]]
    po_free = [bc_ev[0], bc_ev[1], sp_last, sp_last]
    po_ev = [None] * 4
    xr_free = None
    ln1_v = [None] * NTT
    for mm_ in range(4):
        gm = None
        for kh in range(4):
            wc = [kb.cast_done["wout"]]
            sl, gw = kb.wload(d.woutb.ap(), kh * 1024, mm_ * 512, waits=wc)
            for jj in range(8):
                k = kh * 8 + jj
                for tt in range(NTT):
                    w = [gw, g_gate[k]]
                    if k == 0:
                        if po_free[tt] is not None:
                            w.append(po_free[tt])
                        if po_ev[tt] is not None:
                            w.append(po_ev[tt])
                    gm = kb.op("tensor", (lambda sl, jj, k, tt: lambda e: e.matmul(
                        PO[tt][:],
                        m.SILUZ[:, k * T + tt * 128:k * T + (tt + 1) * 128],
                        r.WSTR[:, sl * 4096 + jj * 512:sl * 4096 + (jj + 1) * 512],
                        start=(k == 0), stop=(k == NDC - 1)))(sl, jj, k, tt),
                        waits=w, inc="T")
            kb.slot_rel[sl] = gm
        for tt in range(NTT):
            w = [xr_free] if xr_free is not None else []
            gx = kb.dma("scalar", XRES[:],
                        d.x.ap()[t0 + tt * 128:t0 + (tt + 1) * 128,
                                 mm_ * 512:(mm_ + 1) * 512], waits=w)
            gv = kb.op("vector", (lambda mm_, tt: lambda e:
                       e.scalar_tensor_tensor(
                           r.XNTMB[:, tt * DM + mm_ * 512:tt * DM + (mm_ + 1) * 512],
                           PO[tt][:], 1.0, XRES[:],
                           OP.mult, OP.add,
                           accum_out=r.STATS[:, S1SUM + tt * 4 + mm_:
                                             S1SUM + tt * 4 + mm_ + 1]))(mm_, tt),
                       waits=[gm, gx], inc="V")
            po_ev[tt] = gv
            xr_free = gv
            ln1_v[tt] = gv

    last_gdl = None
    for tt in range(NTT):
        emit_ln(nc, kb, r, r.XNTMB, tt, ln1_v[tt],
                S1SUM, S1SSQ, CMU, CNVAR, CSTD, CRSTD, CNMRS, bf16=True,
                sq_extra=last_gdl)
        gn1 = kb.join_ln_gate
        gdl = kb.op("vector", (lambda tt: lambda e: e.scalar_tensor_tensor(
            r.JUNK[:], r.XNTMB[:, tt * DM:(tt + 1) * DM], 1.0, r.GDIFF[:],
            OP.mult, OP.mult,
            accum_out=r.STATS[:, CDLG + tt:CDLG + tt + 1]))(tt),
            waits=[gn1], inc="V")
        last_gdl = kb.op("vector", (lambda tt: lambda e: e.tensor_single_scalar(
            r.STATS[:, CMASK + tt:CMASK + tt + 1],
            r.STATS[:, CDLG + tt:CDLG + tt + 1], 0.0, OP.is_ge))(tt),
            waits=[gdl], inc="V", drain=True)
        last_gdl = gdl

    kb.end_phase(f"m2_{c}")
    es.close()


def emit_ln(nc, kb, r, XT, tt, v_gate, cSUM, cSSQ, cMU, cNVAR, cSTD, cRSTD,
            cNMRS, bf16, sq_extra=None):
    w = [v_gate] if sq_extra is None else [v_gate, sq_extra]
    gsq = kb.op("scalar", (lambda tt: lambda e: e.activation(
        r.JUNK[:], XT[:, tt * DM:(tt + 1) * DM], AF.Square,
        accum_out=r.STATS[:, cSSQ + tt:cSSQ + tt + 1]))(tt),
        waits=w, inc="S")
    gsq = kb.op("scalar", lambda e: e.drain(), waits=[], inc="S")
    gsm = kb.op("vector", (lambda tt: lambda e: e.reduce_sum(
        r.STATS[:, cMU + tt:cMU + tt + 1],
        r.STATS[:, cSUM + tt * 4:cSUM + (tt + 1) * 4], axis=AX.X))(tt),
        waits=[v_gate], inc="V", drain=True)
    gmu = kb.op("vector", (lambda tt: lambda e: e.tensor_scalar_mul(
        r.STATS[:, cMU + tt:cMU + tt + 1],
        r.STATS[:, cMU + tt:cMU + tt + 1], 1.0 / DM))(tt),
        waits=[gsm], inc="V")
    gnv = kb.op("vector", (lambda tt: lambda e: e.tensor_scalar(
        r.STATS[:, cNVAR + tt:cNVAR + tt + 1],
        r.STATS[:, cSSQ + tt:cSSQ + tt + 1], 1.0 / DM, None, OP.mult))(tt),
        waits=[gsq], inc="V")
    gnv = kb.op("vector", (lambda tt: lambda e: e.tensor_scalar(
        r.STATS[:, cNVAR + tt:cNVAR + tt + 1],
        r.STATS[:, cMU + tt:cMU + tt + 1],
        r.STATS[:, cMU + tt:cMU + tt + 1],
        r.STATS[:, cNVAR + tt:cNVAR + tt + 1], OP.mult, OP.subtract))(tt),
        waits=[gmu, gnv], inc="V")
    gst = kb.op("scalar", (lambda tt: lambda e: e.activation(
        r.STATS[:, cSTD + tt:cSTD + tt + 1],
        r.STATS[:, cNVAR + tt:cNVAR + tt + 1], AF.Sqrt,
        bias=EPS, scale=-1.0))(tt), waits=[gnv], inc="S")
    grs = kb.op("vector", (lambda tt: lambda e: e.reciprocal(
        r.STATS[:, cRSTD + tt:cRSTD + tt + 1],
        r.STATS[:, cSTD + tt:cSTD + tt + 1]))(tt), waits=[gst], inc="V")
    gnm = kb.op("vector", (lambda tt: lambda e: e.tensor_scalar(
        r.STATS[:, cNMRS + tt:cNMRS + tt + 1],
        r.STATS[:, cMU + tt:cMU + tt + 1],
        r.STATS[:, cRSTD + tt:cRSTD + tt + 1], -1.0, OP.mult, OP.mult))(tt),
        waits=[grs], inc="V")
    gn = kb.op("vector", (lambda tt: lambda e: e.tensor_scalar(
        XT[:, tt * DM:(tt + 1) * DM], XT[:, tt * DM:(tt + 1) * DM],
        r.STATS[:, cRSTD + tt:cRSTD + tt + 1],
        r.STATS[:, cNMRS + tt:cNMRS + tt + 1],
        OP.mult, OP.add))(tt),
        waits=[gnm], inc="V")
    kb.join_ln_gate = gn


def emit_f(nc, kb, d, r, c):
    es = ExitStack()
    sb = lambda n, sh, dt: es.enter_context(nc.sbuf_tensor(n, sh, dt))
    XNFM = sb(f"XNFM{c}", [128, NMK * T], BF16)
    HF = sb(f"HF{c}", [128, NDC * T], BF16)
    OUT0 = sb(f"OUT0{c}", [128, NTT * DM], BF16)
    XOUT = sb(f"XOUT{c}", [128, NTT * DM], F32)
    FTMP = sb(f"FTMP{c}", [128, 2 * 512], F32)
    t0 = c * T

    kb.begin_phase()

    PA = [es.enter_context(nc.psum_tensor(f"PAA{i}_{c}", [128, 512], F32))
          for i in range(4)]
    PF = [es.enter_context(nc.psum_tensor(f"PFF{i}_{c}", [128, 512], F32))
          for i in range(4)]

    PTB = [PF[0][:].bitcast(BF16), PF[1][:].bitcast(BF16)]
    xnfm_gate = [None] * NMK
    bank_evac_gate = [None, None]
    bank_fill_gate = [None, None]
    for b in range(8):
        bk = b % 2
        for j in range(8):
            i = b * 8 + j
            mk, tt = i // 4, i % 4
            w = []
            if j == 0 and bank_evac_gate[bk] is not None:
                w.append(bank_evac_gate[bk])
            gt = kb.op("tensor", (lambda bk, j, mk, tt: lambda e: e.transpose(
                PTB[bk][:, j * 128:(j + 1) * 128],
                r.XNTMB[:, tt * DM + mk * 128:tt * DM + (mk + 1) * 128],
                r.IDT[:]))(bk, j, mk, tt), waits=w, inc="T")
        bank_fill_gate[bk] = gt
        ge = None
        for j in range(8):
            i = b * 8 + j
            mk, tt = i // 4, i % 4
            ge = kb.op("scalar", (lambda bk, j, mk, tt: lambda e: e.copy(
                XNFM[:, mk * T + tt * 128:mk * T + (tt + 1) * 128],
                PTB[bk][:, j * 128:(j + 1) * 128]))(bk, j, mk, tt),
                waits=[bank_fill_gate[bk]], inc="S")
            if i % 4 == 3:
                xnfm_gate[mk] = ge
        bank_evac_gate[bk] = ge

    def aside(wsrc, wckey, bias_col, func, bank_free, extra_wait=None):
        h_gate = [None] * NDC
        for g in range(8):
            gm = None
            for half in range(2):
                wc = [kb.cast_done[wckey]]
                sl, gw = kb.wload(wsrc.ap(), half * 1024, g * 512, waits=wc)
                for jj in range(8):
                    mk = half * 8 + jj
                    for fcw in range(4):
                        w = [gw, xnfm_gate[mk]]
                        if mk == 0 and bank_free[fcw] is not None:
                            w = w + [bank_free[fcw]]
                        gm = kb.op("tensor", (lambda sl, jj, fcw, mk: lambda e:
                                   e.matmul(
                                       PA[fcw][:],
                                       r.WSTR[:, sl * 4096 + jj * 512 + fcw * 128:
                                              sl * 4096 + jj * 512 + (fcw + 1) * 128],
                                       XNFM[:, mk * T:(mk + 1) * T],
                                       start=(mk == 0), stop=(mk == NMK - 1)))(
                                           sl, jj, fcw, mk),
                            waits=w, inc="T")
                kb.slot_rel[sl] = gm
            for fcw in range(4):
                fc = g * 4 + fcw
                w = [gm]
                if extra_wait is not None and extra_wait[fc] is not None:
                    w.append(extra_wait[fc])
                gh = kb.op("scalar", (lambda fc, fcw: lambda e: e.activation(
                    HF[:, fc * T:(fc + 1) * T], PA[fcw][:], func,
                    bias=bias_col[:, fc:fc + 1]))(fc, fcw), waits=w, inc="S")
                h_gate[fc] = gh
                bank_free[fcw] = gh
        return h_gate

    def bside(wsrc, wckey, h_gate, evac_fn, pf_free):
        hf_last = [None] * NDC
        for mm_ in range(4):
            gm = None
            for kh in range(4):
                wc = [kb.cast_done[wckey]]
                sl, gw = kb.wload(wsrc.ap(), kh * 1024, mm_ * 512, waits=wc)
                for jj in range(8):
                    k = kh * 8 + jj
                    for tt in range(NTT):
                        w = [gw, h_gate[k]]
                        if k == 0 and pf_free[tt] is not None:
                            w = w + [pf_free[tt]]
                        gm = kb.op("tensor", (lambda sl, jj, k, tt: lambda e:
                                   e.matmul(
                                       PF[tt][:],
                                       HF[:, k * T + tt * 128:k * T + (tt + 1) * 128],
                                       r.WSTR[:, sl * 4096 + jj * 512:
                                              sl * 4096 + (jj + 1) * 512],
                                       start=(k == 0), stop=(k == NDC - 1)))(
                                           sl, jj, k, tt),
                            waits=w, inc="T")
                    if mm_ == 3:
                        hf_last[k] = gm
                kb.slot_rel[sl] = gm
            for tt in range(NTT):
                pf_free[tt] = evac_fn(mm_, tt, gm)
        return hf_last

    bank_free = [None] * 4
    h_gate = aside(d.w0ab, "w0a", r.B0AC, AF.Gelu, bank_free)
    pf_free = [None] * 4

    def evac0(mm_, tt, gate):
        return kb.op("vector", (lambda mm_, tt: lambda e: e.tensor_add(
            OUT0[:, tt * DM + mm_ * 512:tt * DM + (mm_ + 1) * 512], PF[tt][:],
            r.B0BBC[:, mm_ * 512:(mm_ + 1) * 512]))(mm_, tt),
            waits=[gate], inc="V")
    hf_last = bside(d.w0bb, "w0b", h_gate, evac0, pf_free)

    h_gate1 = aside(d.w1ab, "w1a", r.B1AC, AF.Relu, bank_free,
                    extra_wait=hf_last)

    ln2_v = [None] * NTT

    def evac1(mm_, tt, gate):
        g1 = kb.op("vector", (lambda mm_, tt: lambda e: e.tensor_add(
            FTMP[:, 0:512], PF[tt][:],
            r.B1BBC[:, mm_ * 512:(mm_ + 1) * 512]))(mm_, tt),
            waits=[gate], inc="V")
        g2 = kb.op("vector", (lambda mm_, tt: lambda e: e.tensor_sub(
            FTMP[:, 512:1024],
            OUT0[:, tt * DM + mm_ * 512:tt * DM + (mm_ + 1) * 512],
            FTMP[:, 0:512]))(mm_, tt), waits=[g1], inc="V")
        g3 = kb.op("vector", (lambda mm_, tt: lambda e: e.scalar_tensor_tensor(
            FTMP[:, 0:512], FTMP[:, 512:1024],
            r.STATS[:, CMASK + tt:CMASK + tt + 1], FTMP[:, 0:512],
            OP.mult, OP.add))(mm_, tt), waits=[g2], inc="V")
        g4 = kb.op("vector", (lambda mm_, tt: lambda e: e.scalar_tensor_tensor(
            XOUT[:, tt * DM + mm_ * 512:tt * DM + (mm_ + 1) * 512],
            FTMP[:, 0:512], 1.0,
            r.XNTMB[:, tt * DM + mm_ * 512:tt * DM + (mm_ + 1) * 512],
            OP.mult, OP.add,
            accum_out=r.STATS[:, S2SUM + tt * 4 + mm_:
                              S2SUM + tt * 4 + mm_ + 1]))(mm_, tt),
            waits=[g3], inc="V")
        ln2_v[tt] = g4
        return g4
    bside(d.w1bb, "w1b", h_gate1, evac1, pf_free)

    for tt in range(NTT):
        emit_ln(nc, kb, r, XOUT, tt, ln2_v[tt],
                S2SUM, S2SSQ, C2MU, C2NVAR, C2STD, C2RSTD, C2NMRS, bf16=False)
        gn2 = kb.join_ln_gate
        kb.dma("sync", d.out.ap()[t0 + tt * 128:t0 + (tt + 1) * 128, :],
               XOUT[:, tt * DM:(tt + 1) * DM], waits=[gn2])

    kb.end_phase(f"f_{c}")
    es.close()


def build():
    nc = bass.Bass("TRN2", target_bir_lowering=False)
    d = declare_dram(nc)
    with ExitStack() as es:
        kb = KB(nc, es)
        r = alloc_resident(nc, es)
        kb.WSTR = r.WSTR
        nc.const_aps.register(F32, EPS, r.CONSTS[:, 0:1])
        nc.const_aps.register(F32, -1.0, r.CONSTS[:, 1:2])
        nc.const_aps.register(F32, 1.0, r.CONSTS[:, 2:3])
        for c in range(NCH):
            with ExitStack() as mes:
                m = NS()
                sb = lambda n, sh, dt: mes.enter_context(nc.sbuf_tensor(n, sh, dt))
                m.U = sb(f"U{c}", [128, NDC * T], BF16)
                m.SILUZ = sb(f"SILUZ{c}", [128, NDC * T], BF16)
                m.DTRB = sb(f"DTRB{c}", [128, T], BF16)
                m.BCR = sb(f"BCR{c}", [32, T], BF16)
                if c == 0:
                    with nc.sbuf_tensor("XBF0", [128, NMK * T], BF16) as XBF0:
                        emit_prologue(nc, kb, d, r, XBF0)
                        emit_m1(nc, kb, d, r, m, c, XBF0=XBF0)
                else:
                    emit_m1(nc, kb, d, r, m, c)
                emit_m2(nc, kb, d, r, m, c)
            emit_f(nc, kb, d, r, c)
    return nc


_NC_CACHE = None


def kernel(**inputs):
    global _NC_CACHE
    if _NC_CACHE is None:
        _NC_CACHE = build()
    nc = _NC_CACHE

    f32 = np.float32
    asf = lambda a: np.asarray(a, dtype=f32)
    w = {
        "winT": asf(inputs["L0_attn_W_in"]).T,
        "convw": asf(inputs["L0_attn_conv_w"]),
        "convb": asf(inputs["L0_attn_conv_b"]),
        "wxT": asf(inputs["L0_attn_W_x"]).T,
        "wdtT": asf(inputs["L0_attn_W_dt"]).T,
        "bdt": asf(inputs["L0_attn_b_dt"]),
        "alog": asf(inputs["L0_attn_A_log"]),
        "dd": asf(inputs["L0_attn_D"]),
        "woutT": asf(inputs["L0_attn_W_out"]).T,
        "wg": asf(inputs["L0_ffn_Wg"]),
        "w0aT": asf(inputs["L0_ffn_W0a"]).T,
        "b0a": asf(inputs["L0_ffn_b0a"]),
        "w0bT": asf(inputs["L0_ffn_W0b"]).T,
        "b0b": asf(inputs["L0_ffn_b0b"]),
        "w1aT": asf(inputs["L0_ffn_W1a"]).T,
        "b1a": asf(inputs["L0_ffn_b1a"]),
        "w1bT": asf(inputs["L0_ffn_W1b"]).T,
        "b1b": asf(inputs["L0_ffn_b1b"]),
    }
    w = {k: np.ascontiguousarray(v) for k, v in w.items()}
    w["cid"] = np.eye(128, dtype=ml_dtypes.bfloat16)
    w["cones"] = np.ones((1, 128), dtype=f32)
    w["conesb"] = np.ones((1, 128), dtype=ml_dtypes.bfloat16)
    esel = np.zeros((32, 32 * 128), dtype=ml_dtypes.bfloat16)
    for s in range(32):
        esel[s, s * 128:(s + 1) * 128] = 1
    w["esel"] = esel

    x = np.asarray(inputs["L0_attn_x"], dtype=f32)
    in_maps = []
    for i in range(8):
        m = dict(w)
        m["x"] = np.ascontiguousarray(x[i])
        m["xT"] = np.ascontiguousarray(x[i].T)
        in_maps.append(m)

    res = run_bass_kernel_spmd(nc, in_maps, core_ids=list(range(8)))
    return np.stack([np.asarray(r["out"]) for r in res.results]).astype(np.float32)


# version 17 (final)
# speedup vs baseline: 1.0889x; 1.0889x over previous
import numpy as np
import ml_dtypes
from contextlib import ExitStack

import concourse.bass as bass
import concourse.mybir as mybir
from concourse.bass_utils import run_bass_kernel_spmd

F32 = mybir.dt.float32
BF16 = mybir.dt.bfloat16
AF = mybir.ActivationFunctionType
OP = mybir.AluOpType
AX = mybir.AxisListType

L = 2048
DM = 2048
DI = 4096
DS = 16
DCV = 4
DR = 128
FH = 4096
T = 512
NCH = L // T
NTT = T // 128
NMK = DM // 128
NDC = DI // 128
NSLOT = 2
EPS = 1e-5
CB = 4096

ENGS = ("tensor", "vector", "scalar", "gpsimd", "sync")

S1SUM, S1SSQ, CMU, CNVAR, CSTD, CRSTD, CNMRS, CDLG, CMASK = \
    0, 16, 20, 24, 28, 32, 36, 40, 44
S2SUM, S2SSQ, C2MU, C2NVAR, C2STD, C2RSTD, C2NMRS = 48, 64, 68, 72, 76, 80, 84


class KB:

    CUMUL = ("qw", "qo", "qg", "qx", "qc")

    def __init__(self, nc, es):
        self.nc = nc
        self.es = es
        self.C = {k: 0 for k in ("qw", "qo", "qg", "qx", "qc", "qd",
                                 "w0", "w1", "w2",
                                 "T", "V", "S", "P", "Y")}
        self.OWN = {"tensor": "T", "vector": "V", "scalar": "S",
                    "gpsimd": "P", "sync": "Y"}
        self.QUEUE = {"sync": "qw", "scalar": "qo", "gpsimd": "qg"}
        self.ENG_DMA = {"sync": ["qw", "qx"] + [f"w{i}" for i in range(NSLOT)],
                        "scalar": ["qo"] + [f"w{i}" for i in range(NSLOT)],
                        "gpsimd": ["qg"]}
        self.waited = {e: {} for e in ENGS}
        self.sem = {}
        self.q = None
        self.join_gate = {e: None for e in ENGS}
        for k in self.C:
            self.sem[k] = es.enter_context(nc.semaphore("sem_" + k))
        self.slot_rel = [None] * NSLOT
        self.widx = 0
        self.WSTR = None
        self.cast_done = {}

    def _waits(self, engname, waits):
        wl = []
        for s, v in waits:
            if v is None:
                continue
            if isinstance(v, tuple):
                raise TypeError(f"bad gate {s} {v}")
            if v <= 0:
                continue
            assert v <= self.C[s], f"wait on future {s}={v} > {self.C[s]}"
            if s in self.CUMUL:
                v = self.C[s]
            if self.OWN[engname] == s:
                continue
            if self.waited[engname].get(s, -1) >= v:
                continue
            self.waited[engname][s] = v
            wl.append((s, v))
        return wl

    def op(self, engname, fn, waits=(), inc=None, inc_n=1, drain=False):
        wl = self._waits(engname, waits)
        gate = None
        if inc is not None:
            self.C[inc] += inc_n
            gate = (inc, self.C[inc])
        sems = self.sem

        def run(eng):
            if drain:
                eng.drain()
            for s, v in wl:
                eng.wait_ge(sems[s], v)
            ins = fn(eng)
            if inc is not None:
                if ins is None:
                    ins = eng.nop()
                ins.then_inc(sems[inc], inc_n)
        self.q[engname].append(run)
        return gate

    def dma(self, engname, out_ap, in_ap, waits=(), allow_slow=False,
            qsem=None):
        qs = qsem or self.QUEUE[engname]

        def fn(eng):
            if allow_slow:
                return eng.dma_start(out_ap, in_ap,
                                     allow_slow_non_contiguous=True)
            return eng.dma_start(out_ap, in_ap)
        return self.op(engname, fn, waits=waits, inc=qs, inc_n=16)

    def wload(self, src2d, r0, c0, waits=()):
        sl = self.widx % NSLOT
        eng = "sync" if (self.widx % 2 == 0) else "scalar"
        w = list(waits)
        if self.slot_rel[sl] is not None:
            w.append(self.slot_rel[sl])
        src = src2d[r0:r0 + 1024, c0:c0 + 512].rearrange(
            "(m p) f -> p m f", p=128)
        dst = self.WSTR[:, sl * 4096:(sl + 1) * 4096].rearrange(
            "p (m f) -> p m f", m=8)
        g = self.dma(eng, dst, src, waits=w, qsem=f"w{sl}")
        self.widx += 1
        return sl, g

    def emit_cast(self, src, dst, name, qsem="qc", pieces=2):
        rows = src.shape[0]
        assert rows % pieces == 0
        rp = rows // pieces
        g = None
        for i in range(pieces):
            g = self.dma("gpsimd", dst.ap()[i * rp:(i + 1) * rp, :],
                         src.ap()[i * rp:(i + 1) * rp, :], qsem=qsem)
        self.cast_done[name] = g

    def begin_phase(self):
        self.q = {e: [] for e in ENGS}
        gates = [g for g in self.join_gate.values() if g is not None]
        for e in ENGS:
            wl = self._waits(e, gates)
            if wl:
                sems = self.sem

                def mk(wl):
                    def run(eng):
                        for s, v in wl:
                            eng.wait_ge(sems[s], v)
                    return run
                self.q[e].append(mk(wl))

    def end_phase(self, name):
        for e in ENGS:
            waits = []
            for qs in self.ENG_DMA.get(e, ()):
                if self.C[qs] > 0:
                    waits.append((qs, self.C[qs]))
            self.join_gate[e] = self.op(e, lambda eng: eng.drain(),
                                        waits=waits, inc=self.OWN[e])
        qq = self.q
        with self.nc.Block(name) as blk:
            @blk.tensor
            def _(eng):
                for f in qq["tensor"]:
                    f(eng)

            @blk.vector
            def _(eng):
                for f in qq["vector"]:
                    f(eng)

            @blk.scalar
            def _(eng):
                for f in qq["scalar"]:
                    f(eng)

            @blk.gpsimd
            def _(eng):
                for f in qq["gpsimd"]:
                    f(eng)

            @blk.sync
            def _(eng):
                for f in qq["sync"]:
                    f(eng)
        self.q = None


class NS:
    pass


def declare_dram(nc):
    d = NS()
    d.x = nc.dram_tensor("x", [L, DM], F32, kind="ExternalInput")
    d.xT = nc.dram_tensor("xT", [DM, L], F32, kind="ExternalInput")
    d.winT = nc.dram_tensor("winT", [DM, 2 * DI], F32, kind="ExternalInput")
    d.convw = nc.dram_tensor("convw", [DI, DCV], F32, kind="ExternalInput")
    d.convb = nc.dram_tensor("convb", [DI], F32, kind="ExternalInput")
    d.wxT = nc.dram_tensor("wxT", [DI, DR + 2 * DS], F32, kind="ExternalInput")
    d.wdtT = nc.dram_tensor("wdtT", [DR, DI], F32, kind="ExternalInput")
    d.bdt = nc.dram_tensor("bdt", [DI], F32, kind="ExternalInput")
    d.alog = nc.dram_tensor("alog", [DI, DS], F32, kind="ExternalInput")
    d.dd = nc.dram_tensor("dd", [DI], F32, kind="ExternalInput")
    d.woutT = nc.dram_tensor("woutT", [DI, DM], F32, kind="ExternalInput")
    d.wg = nc.dram_tensor("wg", [2, DM], F32, kind="ExternalInput")
    d.w0aT = nc.dram_tensor("w0aT", [DM, FH], F32, kind="ExternalInput")
    d.b0a = nc.dram_tensor("b0a", [FH], F32, kind="ExternalInput")
    d.w0bT = nc.dram_tensor("w0bT", [FH, DM], F32, kind="ExternalInput")
    d.b0b = nc.dram_tensor("b0b", [DM], F32, kind="ExternalInput")
    d.w1aT = nc.dram_tensor("w1aT", [DM, FH], F32, kind="ExternalInput")
    d.b1a = nc.dram_tensor("b1a", [FH], F32, kind="ExternalInput")
    d.w1bT = nc.dram_tensor("w1bT", [FH, DM], F32, kind="ExternalInput")
    d.b1b = nc.dram_tensor("b1b", [DM], F32, kind="ExternalInput")
    d.cid = nc.dram_tensor("cid", [128, 128], BF16, kind="ExternalInput")
    d.cones = nc.dram_tensor("cones", [1, 128], F32, kind="ExternalInput")
    d.cid32f = nc.dram_tensor("cid32f", [32, 32], F32, kind="ExternalInput")
    d.esel = nc.dram_tensor("esel", [32, 32 * 128], BF16, kind="ExternalInput")
    d.out = nc.dram_tensor("out", [L, DM], F32, kind="ExternalOutput")
    d.winb = nc.dram_tensor("winb_s", [DM, 2 * DI], BF16)
    d.woutb = nc.dram_tensor("woutb_s", [DI, DM], BF16)
    d.w0ab = nc.dram_tensor("w0ab_s", [DM, FH], BF16)
    d.w0bb = nc.dram_tensor("w0bb_s", [FH, DM], BF16)
    d.w1ab = nc.dram_tensor("w1ab_s", [DM, FH], BF16)
    d.w1bb = nc.dram_tensor("w1bb_s", [FH, DM], BF16)
    return d


def alloc_resident(nc, es):
    r = NS()
    sb = lambda n, sh, dt: es.enter_context(nc.sbuf_tensor(n, sh, dt))
    r.WXT = sb("WXT", [128, NDC * 160], BF16)
    r.WDTT = sb("WDTT", [128, DI], BF16)
    r.ANEG = sb("ANEG", [128, NDC * DS], F32)
    r.DCOL = sb("DCOL", [128, NDC], F32)
    r.CONVB = sb("CONVB", [128, NDC], F32)
    r.BDTC = sb("BDTC", [128, NDC], F32)
    r.B0AC = sb("B0AC", [128, NDC], F32)
    r.B1AC = sb("B1AC", [128, NDC], F32)
    r.CONVW = sb("CONVW", [128, NDC * DCV], F32)
    r.GDIFF = sb("GDIFF", [128, DM], F32)
    r.B0BBC = sb("B0BBC", [128, DM], BF16)
    r.B1BBC = sb("B1BBC", [128, DM], BF16)
    r.IDT = sb("IDT", [128, 128], BF16)
    r.ONESF = sb("ONESF", [1, 128], F32)
    r.ID32F = sb("ID32F", [32, 32], F32)
    r.ESEL = sb("ESEL", [32, 32 * 128], BF16)
    r.CONSTS = sb("CONSTS", [128, 4], F32)
    r.HCARRY = sb("HCARRY", [128, NDC * DS], F32)
    r.TAILS = sb("TAILS", [128, NDC * 3], F32)
    r.STATS = sb("STATS", [128, 96], F32)
    r.JUNK = sb("JUNK", [128, DM], BF16)
    r.XNTMB = sb("XNTMB", [128, NTT * DM], BF16)
    r.WSTR = sb("WSTR", [128, NSLOT * 4096], BF16)
    return r


def emit_prologue(nc, kb, d, r, XBF0):
    es = ExitStack()
    sb = lambda n, sh, dt: es.enter_context(nc.sbuf_tensor(n, sh, dt))
    PROW = sb("PROW", [1, 2 * DM], F32)
    PPA = es.enter_context(nc.psum_tensor("PPA", [128, 512], F32))

    kb.begin_phase()

    for mk in range(NMK):
        kb.dma("gpsimd", XBF0[:, mk * T:(mk + 1) * T],
               d.xT.ap()[mk * 128:(mk + 1) * 128, 0:T], qsem="qx")
    for dc in range(NDC):
        kb.dma("gpsimd", r.WXT[:, dc * 160:(dc + 1) * 160],
               d.wxT.ap()[dc * 128:(dc + 1) * 128, :], qsem="qg")
    kb.dma("gpsimd", r.WDTT[:], d.wdtT.ap(), qsem="qg")

    kb.emit_cast(d.winT, d.winb, "win", qsem="qd", pieces=4)
    kb.emit_cast(d.woutT, d.woutb, "wout")
    kb.emit_cast(d.w0aT, d.w0ab, "w0a")
    kb.emit_cast(d.w0bT, d.w0bb, "w0b")
    kb.emit_cast(d.w1aT, d.w1ab, "w1a")
    kb.emit_cast(d.w1bT, d.w1bb, "w1b")

    g_wg = kb.dma("sync", PROW[:].rearrange("p (r m) -> p r m", r=2),
                  d.wg.ap().unsqueeze(0))
    g_of = kb.dma("sync", r.ONESF[:], d.cones.ap())
    g_i32 = kb.dma("sync", r.ID32F[:], d.cid32f.ap())
    VROW = sb("VROW", [32, 5 * 128], F32)
    vjobs = ((d.dd, r.DCOL), (d.convb, r.CONVB), (d.bdt, r.BDTC),
             (d.b0a, r.B0AC), (d.b1a, r.B1AC))
    g_vr = None
    for i, (vec, dst) in enumerate(vjobs):
        g_vr = kb.dma("sync", VROW[:, i * 128:(i + 1) * 128],
                      vec.ap().rearrange("(r q) -> r q", q=128))
    g_alog = None
    for dc in range(NDC):
        g_alog = kb.dma("sync", r.ANEG[:, dc * DS:(dc + 1) * DS],
                        d.alog.ap()[dc * 128:(dc + 1) * 128, :])
    for dc in range(NDC):
        kb.dma("sync", r.CONVW[:, dc * DCV:(dc + 1) * DCV],
               d.convw.ap()[dc * 128:(dc + 1) * 128, :])
    kb.dma("sync", r.IDT[:], d.cid.ap())
    kb.dma("sync", r.ESEL[:], d.esel.ap())

    for i in range(5):
        kb.op("tensor", (lambda i: lambda e: e.matmul(
            PPA[:, i * 32:(i + 1) * 32], VROW[:, i * 128:(i + 1) * 128],
            r.ID32F[:], start=True, stop=True))(i),
            waits=[g_vr, g_i32], inc="T")
    g_tp = ("T", kb.C["T"])
    g_ve = None
    for i, (vec, dst) in enumerate(vjobs):
        g_ve = kb.op("scalar", (lambda i, dst: lambda e: e.copy(
            dst[:], PPA[:, i * 32:(i + 1) * 32]))(i, dst),
            waits=[g_tp], inc="S")

    gA = kb.op("scalar", lambda e: e.activation(r.ANEG[:], r.ANEG[:], AF.Exp),
               waits=[g_alog], inc="S")
    kb.op("scalar", lambda e: e.mul(r.ANEG[:], r.ANEG[:], -1.0),
          waits=[gA], inc="S")

    kb.op("vector", lambda e: e.memset(r.CONSTS[:, 0:1], EPS), inc="V")
    kb.op("vector", lambda e: e.memset(r.CONSTS[:, 1:2], -1.0), inc="V")
    kb.op("vector", lambda e: e.memset(r.CONSTS[:, 2:3], 1.0), inc="V")
    kb.op("vector", lambda e: e.memset(r.HCARRY[:], 0.0), inc="V")
    kb.op("vector", lambda e: e.memset(r.TAILS[:], 0.0), inc="V")

    g_gs = kb.op("vector", lambda e: e.tensor_sub(
        PROW[0:1, 0:DM], PROW[0:1, 0:DM], PROW[0:1, DM:2 * DM]),
        waits=[g_wg], inc="V")

    ev = None
    gm = None
    for m in range(4):
        w = [g_of, g_gs, g_ve]
        if ev is not None:
            w.append(ev)
        gm = kb.op("tensor", (lambda m: lambda e: e.matmul(
            PPA[:], r.ONESF[0:1, :], PROW[0:1, m * 512:(m + 1) * 512],
            start=True, stop=True))(m), waits=w, inc="T")
        ev = kb.op("vector", (lambda m: lambda e: e.tensor_copy(
            r.GDIFF[:, m * 512:(m + 1) * 512], PPA[:]))(m),
            waits=[gm], inc="V")
    g_b0b = kb.dma("sync", PROW[0:1, 0:DM], d.b0b.ap().unsqueeze(0),
                   waits=[gm])
    g_b1b = kb.dma("sync", PROW[0:1, DM:2 * DM], d.b1b.ap().unsqueeze(0),
                   waits=[g_gs])
    for row_ap, dst, gsrc in ((PROW[0:1, 0:DM], r.B0BBC, g_b0b),
                              (PROW[0:1, DM:2 * DM], r.B1BBC, g_b1b)):
        for m in range(4):
            w = [g_of, gsrc, ev]
            gm = kb.op("tensor", (lambda ra, m: lambda e: e.matmul(
                PPA[:], r.ONESF[0:1, :], ra[:, m * 512:(m + 1) * 512],
                start=True, stop=True))(row_ap, m), waits=w, inc="T")
            ev = kb.op("vector", (lambda dst, m: lambda e: e.tensor_copy(
                dst[:, m * 512:(m + 1) * 512], PPA[:]))(dst, m),
                waits=[gm], inc="V")

    kb.end_phase("prologue")
    es.close()


def emit_m1(nc, kb, d, r, m, c, XBF0=None):
    es = ExitStack()
    sb = lambda n, sh, dt: es.enter_context(nc.sbuf_tensor(n, sh, dt))
    XBF = XBF0 if XBF0 is not None else sb(f"XBF{c}", [128, NMK * T], BF16)
    XSW = sb(f"XSW{c}", [128, 2 * 520], F32)
    ACC = sb(f"ACC{c}", [128, 2 * T], F32)
    PG = [es.enter_context(nc.psum_tensor(f"PG{i}_{c}", [128, 512], F32))
          for i in range(4)]
    PXD = es.enter_context(nc.psum_tensor(f"PXD{c}", [128, 512], F32))
    PXBC = es.enter_context(nc.psum_tensor(f"PXBC{c}", [128, 512], F32))
    t0 = c * T

    kb.begin_phase()

    if XBF0 is not None:
        xbf_gate = [("qx", 16 * (mk + 1)) for mk in range(NMK)]
    else:
        xbf_gate = []
        for mk in range(NMK):
            xbf_gate.append(kb.dma(
                "gpsimd", XBF[:, mk * T:(mk + 1) * T],
                d.xT.ap()[mk * 128:(mk + 1) * 128, t0:t0 + T], qsem="qx"))

    xs_ev = [None] * 4
    z_ev = [None] * 4
    u_gate = [None] * NDC
    silu_gate = [None, None]
    xdbl_pending = []
    xdbl_last = [None]

    def emit_xdbl(fc):
        kb.op("tensor", (lambda fc: lambda e: e.matmul(
            PXD[:], r.WXT[:, fc * 160:fc * 160 + 128], m.U[:, fc * T:(fc + 1) * T],
            start=(fc == 0), stop=(fc == NDC - 1)))(fc),
            waits=[u_gate[fc]], inc="T")
        xdbl_last[0] = kb.op("tensor", (lambda fc: lambda e: e.matmul(
            PXBC[0:32, :], r.WXT[:, fc * 160 + 128:fc * 160 + 160],
            m.U[:, fc * T:(fc + 1) * T],
            start=(fc == 0), stop=(fc == NDC - 1)))(fc), waits=[], inc="T")

    for g in range(16):
        is_z = g >= 8
        gm = None
        for half in range(2):
            wc = [kb.cast_done["win"]]
            sl, gw = kb.wload(d.winb.ap(), half * 1024, g * 512, waits=wc)
            for jj in range(8):
                mk = half * 8 + jj
                for fcw in range(4):
                    w = [gw, xbf_gate[mk]]
                    if mk == 0:
                        eg = (z_ev if is_z else xs_ev)[fcw]
                        if eg is not None:
                            w.append(eg)
                        if is_z and g == 8 and xs_ev[fcw] is not None:
                            w.append(xs_ev[fcw])
                    gm = kb.op("tensor", (lambda sl, jj, fcw, mk: lambda e: e.matmul(
                        PG[fcw][:],
                        r.WSTR[:, sl * 4096 + jj * 512 + fcw * 128:
                               sl * 4096 + jj * 512 + (fcw + 1) * 128],
                        XBF[:, mk * T:(mk + 1) * T],
                        start=(mk == 0), stop=(mk == NMK - 1)))(sl, jj, fcw, mk),
                        waits=w, inc="T")
            kb.slot_rel[sl] = gm
        grp_done = gm

        if not is_z:
            for fcw in range(4):
                fc = g * 4 + fcw
                bi = fc % 2
                w = [grp_done]
                if silu_gate[bi] is not None:
                    w.append(silu_gate[bi])
                kb.op("scalar", (lambda bi, fc: lambda e: e.copy(
                    XSW[:, bi * 520:bi * 520 + 3],
                    r.TAILS[:, fc * 3:fc * 3 + 3]))(bi, fc), waits=w, inc="S")
                kb.op("scalar", (lambda bi, fcw: lambda e: e.copy(
                    XSW[:, bi * 520 + 3:bi * 520 + 3 + T], PG[fcw][:]))(bi, fcw),
                    waits=[], inc="S")
                gt3 = kb.op("scalar", (lambda fc, fcw: lambda e: e.copy(
                    r.TAILS[:, fc * 3:fc * 3 + 3], PG[fcw][:, T - 3:T]))(fc, fcw),
                    waits=[], inc="S")
                xs_ev[fcw] = gt3
                gcv = kb.op("vector", (lambda bi, fc: lambda e: e.tensor_scalar(
                    ACC[:, bi * T:(bi + 1) * T], XSW[:, bi * 520:bi * 520 + T],
                    r.CONVW[:, fc * 4:fc * 4 + 1], r.CONVB[:, fc:fc + 1],
                    OP.mult, OP.add))(bi, fc), waits=[gt3], inc="V")
                for k in range(1, DCV):
                    gcv = kb.op("vector", (lambda bi, fc, k: lambda e:
                                e.scalar_tensor_tensor(
                                    ACC[:, bi * T:(bi + 1) * T],
                                    XSW[:, bi * 520 + k:bi * 520 + k + T],
                                    r.CONVW[:, fc * 4 + k:fc * 4 + k + 1],
                                    ACC[:, bi * T:(bi + 1) * T],
                                    OP.mult, OP.add))(bi, fc, k),
                                waits=[], inc="V")
                gu = kb.op("scalar", (lambda bi, fc: lambda e: e.activation(
                    m.U[:, fc * T:(fc + 1) * T], ACC[:, bi * T:(bi + 1) * T],
                    AF.Silu))(bi, fc), waits=[gcv], inc="S")
                silu_gate[bi] = gu
                u_gate[fc] = gu
                xdbl_pending.append(fc)
            if g >= 1:
                for fc in xdbl_pending[:4]:
                    emit_xdbl(fc)
                xdbl_pending[:4] = []
        else:
            if g == 8:
                for fc in xdbl_pending:
                    emit_xdbl(fc)
                xdbl_pending.clear()
            for fcw in range(4):
                zc = (g - 8) * 4 + fcw
                z_ev[fcw] = kb.op("scalar", (lambda zc, fcw: lambda e:
                                  e.activation(m.SILUZ[:, zc * T:(zc + 1) * T],
                                               PG[fcw][:], AF.Silu))(zc, fcw),
                                  waits=[grp_done], inc="S")

    kb.op("scalar", lambda e: e.copy(m.DTRB[:], PXD[:]),
          waits=[xdbl_last[0]], inc="S")
    kb.op("scalar", lambda e: e.copy(m.BCR[:], PXBC[0:32, :]), waits=[], inc="S")

    kb.end_phase(f"m1_{c}")
    es.close()


def emit_m2(nc, kb, d, r, m, c):
    es = ExitStack()
    sb = lambda n, sh, dt: es.enter_context(nc.sbuf_tensor(n, sh, dt))
    BBC = sb(f"BBC{c}", [128, DS * T], BF16)
    CBC = sb(f"CBC{c}", [128, DS * T], BF16)
    YMM = sb(f"YMM{c}", [128, T], BF16)
    YACC = sb(f"YACC{c}", [128, T], BF16)
    DTT = sb(f"DTT{c}", [128, T], BF16)
    DUU2 = sb(f"DUU2{c}", [128, 2 * T], BF16)
    DAA = sb(f"DAA{c}", [128, 4 * T], BF16)
    DBU = sb(f"DBU{c}", [128, 4 * T], BF16)
    HSSW = sb(f"HSSW{c}", [128, 8 * T], BF16)
    XRES = sb(f"XRES{c}", [128, 512], F32)
    PB = [es.enter_context(nc.psum_tensor(f"PBB{i}_{c}", [128, 512], F32))
          for i in range(2)]
    PW = [es.enter_context(nc.psum_tensor(f"PWW{i}_{c}", [128, 512], F32))
          for i in range(2)]
    t0 = c * T

    kb.begin_phase()

    bc_ev = [None, None]
    bbc_gate = [None] * DS
    cbc_gate = [None] * DS
    for i in range(2 * DS):
        isC = i >= DS
        s = i % DS
        pb = PB[i % 2]
        w = []
        if bc_ev[i % 2] is not None:
            w.append(bc_ev[i % 2])
        gm = kb.op("tensor", (lambda pb, i: lambda e: e.matmul(
            pb[:], r.ESEL[:, i * 128:(i + 1) * 128], m.BCR[:],
            start=True, stop=True))(pb, i), waits=w, inc="T")
        dst = CBC if isC else BBC
        ge = kb.op("scalar", (lambda pb, dst, s: lambda e: e.copy(
            dst[:, s * T:(s + 1) * T], pb[:]))(pb, dst, s),
            waits=[gm], inc="S")
        bc_ev[i % 2] = ge
        (cbc_gate if isC else bbc_gate)[s] = ge

    pw_ev = [None, None]
    da_slot = [None] * 4
    dbu_rd = [None, None]
    g_gate = [None] * NDC
    sp_last = None
    last_carry = None
    st = NS()
    st.gsp = {}
    st.gsp5_prev = None

    def emit_prep_a(dc):
        pw = PW[dc % 2]
        w = [pw_ev[dc % 2]] if pw_ev[dc % 2] is not None else []
        gm = kb.op("tensor", (lambda pw, dc: lambda e: e.matmul(
            pw[:], r.WDTT[:, dc * 128:(dc + 1) * 128], m.DTRB[:],
            start=True, stop=True))(pw, dc), waits=w, inc="T")
        w2 = [gm]
        if st.gsp5_prev is not None:
            w2.append(st.gsp5_prev)
        kb.op("scalar", (lambda pw, dc: lambda e: e.activation(
            r.JUNK[:, 0:T], pw[:], AF.Abs,
            bias=r.BDTC[:, dc:dc + 1]))(pw, dc), waits=w2, inc="S")
        kb.op("scalar", lambda e: e.activation(
            r.JUNK[:, 0:T], r.JUNK[:, 0:T], AF.Exp, scale=-1.0), inc="S")
        kb.op("scalar", lambda e: e.activation(
            r.JUNK[:, 0:T], r.JUNK[:, 0:T], AF.Ln, bias=1.0), inc="S")
        gsp = kb.op("scalar", (lambda pw, dc: lambda e: e.activation(
            r.JUNK[:, T:2 * T], pw[:], AF.Relu,
            bias=r.BDTC[:, dc:dc + 1]))(pw, dc), waits=[], inc="S")
        pw_ev[dc % 2] = gsp
        st.gsp[dc] = gsp

    def emit_prep_b(dc):
        gsp5 = kb.op("vector", lambda e: e.tensor_add(
            DTT[:], r.JUNK[:, 0:T], r.JUNK[:, T:2 * T]),
            waits=[st.gsp[dc]], inc="V")
        st.gsp5_prev = gsp5
        w2 = [gsp5] + [g for g in dbu_rd if g is not None]
        gdu = kb.op("vector", (lambda dc: lambda e: e.tensor_mul(
            DUU2[:, 0:T], DTT[:],
            m.U[:, dc * T:(dc + 1) * T]))(dc), waits=w2, inc="V")
        gcp = kb.op("scalar", lambda e: e.copy(
            DUU2[:, T:2 * T], DUU2[:, 0:T]), waits=[gdu], inc="S")
        return gsp5, gcp

    X = r.XNTMB
    emit_prep_a(0)
    nxt = emit_prep_b(0)
    for dc in range(NDC):
        gsp5, gcp = nxt
        sp_last = st.gsp[dc]
        carry_h0 = None
        for q in range(8):
            wsl = q % 2
            w = [gcp, bbc_gate[2 * q + 1]]
            if dbu_rd[wsl] is not None:
                w.append(dbu_rd[wsl])
            gdb = kb.op("vector", (lambda wsl, q: lambda e: e.tensor_mul(
                DBU[:, wsl * 2 * T:(wsl + 1) * 2 * T], DUU2[:],
                BBC[:, 2 * q * T:(2 * q + 2) * T]))(wsl, q),
                waits=w, inc="V")
            for i2 in range(2):
                s = 2 * q + i2
                j = s % 4
                sl8 = s % 8
                h = s // 8
                col = dc * DS + s
                w = [gsp5]
                if da_slot[j] is not None:
                    w.append(da_slot[j])
                gda = kb.op("scalar", (lambda j, col: lambda e: e.activation(
                    DAA[:, j * T:(j + 1) * T],
                    DTT[:], AF.Exp,
                    scale=r.ANEG[:, col:col + 1]))(j, col), waits=w, inc="S")
                if s == DS - 1 and dc + 1 < NDC:
                    emit_prep_a(dc + 1)
                w = [gda, gdb]
                if h == 0 and last_carry is not None:
                    w.append(last_carry)
                if h == 1 and carry_h0 is not None:
                    w.append(carry_h0)
                gsc = kb.op("vector", (lambda sl8, j, wsl, i2, col: lambda e:
                            e.tensor_tensor_scan(
                                HSSW[:, sl8 * T:(sl8 + 1) * T],
                                DAA[:, j * T:(j + 1) * T],
                                DBU[:, (wsl * 2 + i2) * T:
                                    (wsl * 2 + i2 + 1) * T],
                                r.HCARRY[:, col:col + 1],
                                OP.mult, OP.add))(sl8, j, wsl, i2, col),
                            waits=w, inc="V")
                da_slot[j] = gsc
                if i2 == 1:
                    dbu_rd[wsl] = gsc
                if sl8 == 7:
                    c0 = dc * DS + h * 8
                    gcr = kb.op("scalar", (lambda c0: lambda e: e.copy(
                        r.HCARRY[:, c0:c0 + 8],
                        HSSW[:].rearrange("p (s t) -> p s t", s=8)
                        [:, :, T - 1:T].rearrange("p s o -> p (s o)")))(c0),
                        waits=[gsc], inc="S")
                    if h == 0:
                        carry_h0 = gcr
                    else:
                        last_carry = gcr
                    gwm = kb.op("vector", (lambda h: lambda e: e.tensor_mul(
                        X[:, h * 4096:(h + 1) * 4096], HSSW[:],
                        CBC[:, h * 8 * T:(h + 1) * 8 * T]))(h),
                        waits=[gsc, cbc_gate[h * 8 + 7]], inc="V")
                    if h == 0:
                        kb.op("vector", lambda e: e.tensor_add(
                            X[:, 4096:6144], X[:, 0:2048], X[:, 2048:4096]),
                            waits=[gwm], inc="V")
                        kb.op("vector", lambda e: e.tensor_add(
                            X[:, 6144:7168], X[:, 4096:5120], X[:, 5120:6144]),
                            inc="V")
                        gy0 = kb.op("vector", lambda e: e.tensor_add(
                            YACC[:], X[:, 6144:6656], X[:, 6656:7168]),
                            inc="V")
                    else:
                        if dc + 1 < NDC:
                            nxt = emit_prep_b(dc + 1)
                        kb.op("vector", lambda e: e.tensor_add(
                            X[:, 0:2048], X[:, 4096:6144], X[:, 6144:8192]),
                            waits=[gwm], inc="V")
                        kb.op("vector", lambda e: e.tensor_add(
                            X[:, 2048:3072], X[:, 0:1024], X[:, 1024:2048]),
                            inc="V")
                        gy1 = kb.op("vector", lambda e: e.tensor_add(
                            YMM[:], X[:, 2048:2560], X[:, 2560:3072]),
                            inc="V")
        gyd = kb.op("vector", (lambda dc: lambda e: e.scalar_tensor_tensor(
            YACC[:], m.U[:, dc * T:(dc + 1) * T],
            r.DCOL[:, dc:dc + 1], YACC[:],
            OP.mult, OP.add))(dc), waits=[gy0], inc="V")
        gfin = kb.op("vector", lambda e: e.tensor_add(
            HSSW[:, 0:T], YACC[:], YMM[:]),
            waits=[gyd, gy1, last_carry], inc="V")
        g_gate[dc] = kb.op("vector", (lambda dc: lambda e: e.tensor_mul(
            m.SILUZ[:, dc * T:(dc + 1) * T], m.SILUZ[:, dc * T:(dc + 1) * T],
            HSSW[:, 0:T]))(dc), waits=[gfin], inc="V")

    PO = [PB[0], PB[1], PW[0], PW[1]]
    po_free = [bc_ev[0], bc_ev[1], sp_last, sp_last]
    po_ev = [None] * 4
    xr_free = None
    ln1_v = [None] * NTT
    for mm_ in range(4):
        gm = None
        for kh in range(4):
            wc = [kb.cast_done["wout"]]
            sl, gw = kb.wload(d.woutb.ap(), kh * 1024, mm_ * 512, waits=wc)
            for jj in range(8):
                k = kh * 8 + jj
                for tt in range(NTT):
                    w = [gw, g_gate[k]]
                    if k == 0:
                        if po_free[tt] is not None:
                            w.append(po_free[tt])
                        if po_ev[tt] is not None:
                            w.append(po_ev[tt])
                    gm = kb.op("tensor", (lambda sl, jj, k, tt: lambda e: e.matmul(
                        PO[tt][:],
                        m.SILUZ[:, k * T + tt * 128:k * T + (tt + 1) * 128],
                        r.WSTR[:, sl * 4096 + jj * 512:sl * 4096 + (jj + 1) * 512],
                        start=(k == 0), stop=(k == NDC - 1)))(sl, jj, k, tt),
                        waits=w, inc="T")
            kb.slot_rel[sl] = gm
        for tt in range(NTT):
            w = [xr_free] if xr_free is not None else []
            gx = kb.dma("scalar", XRES[:],
                        d.x.ap()[t0 + tt * 128:t0 + (tt + 1) * 128,
                                 mm_ * 512:(mm_ + 1) * 512], waits=w)
            gv = kb.op("vector", (lambda mm_, tt: lambda e:
                       e.scalar_tensor_tensor(
                           r.XNTMB[:, tt * DM + mm_ * 512:tt * DM + (mm_ + 1) * 512],
                           PO[tt][:], 1.0, XRES[:],
                           OP.mult, OP.add,
                           accum_out=r.STATS[:, S1SUM + tt * 4 + mm_:
                                             S1SUM + tt * 4 + mm_ + 1]))(mm_, tt),
                       waits=[gm, gx], inc="V")
            po_ev[tt] = gv
            xr_free = gv
            ln1_v[tt] = gv

    last_gdl = None
    for tt in range(NTT):
        emit_ln(nc, kb, r, r.XNTMB, tt, ln1_v[tt],
                S1SUM, S1SSQ, CMU, CNVAR, CSTD, CRSTD, CNMRS, bf16=True,
                sq_extra=last_gdl)
        gn1 = kb.join_ln_gate
        gdl = kb.op("vector", (lambda tt: lambda e: e.scalar_tensor_tensor(
            r.JUNK[:], r.XNTMB[:, tt * DM:(tt + 1) * DM], 1.0, r.GDIFF[:],
            OP.mult, OP.mult,
            accum_out=r.STATS[:, CDLG + tt:CDLG + tt + 1]))(tt),
            waits=[gn1], inc="V")
        last_gdl = kb.op("vector", (lambda tt: lambda e: e.tensor_single_scalar(
            r.STATS[:, CMASK + tt:CMASK + tt + 1],
            r.STATS[:, CDLG + tt:CDLG + tt + 1], 0.0, OP.is_ge))(tt),
            waits=[gdl], inc="V", drain=True)
        last_gdl = gdl

    kb.end_phase(f"m2_{c}")
    es.close()


def emit_ln(nc, kb, r, XT, tt, v_gate, cSUM, cSSQ, cMU, cNVAR, cSTD, cRSTD,
            cNMRS, bf16, sq_extra=None):
    w = [v_gate] if sq_extra is None else [v_gate, sq_extra]
    gsq = kb.op("vector", (lambda tt: lambda e: e.scalar_tensor_tensor(
        r.JUNK[:], XT[:, tt * DM:(tt + 1) * DM], 1.0,
        XT[:, tt * DM:(tt + 1) * DM], OP.mult, OP.mult,
        accum_out=r.STATS[:, cSSQ + tt:cSSQ + tt + 1]))(tt),
        waits=w, inc="V")
    gsm = kb.op("vector", (lambda tt: lambda e: e.reduce_sum(
        r.STATS[:, cMU + tt:cMU + tt + 1],
        r.STATS[:, cSUM + tt * 4:cSUM + (tt + 1) * 4], axis=AX.X))(tt),
        waits=[v_gate], inc="V", drain=True)
    gmu = kb.op("vector", (lambda tt: lambda e: e.tensor_scalar_mul(
        r.STATS[:, cMU + tt:cMU + tt + 1],
        r.STATS[:, cMU + tt:cMU + tt + 1], 1.0 / DM))(tt),
        waits=[gsm], inc="V")
    gnv = kb.op("vector", (lambda tt: lambda e: e.tensor_scalar(
        r.STATS[:, cNVAR + tt:cNVAR + tt + 1],
        r.STATS[:, cSSQ + tt:cSSQ + tt + 1], 1.0 / DM, None, OP.mult))(tt),
        waits=[gsq], inc="V")
    gnv = kb.op("vector", (lambda tt: lambda e: e.tensor_scalar(
        r.STATS[:, cNVAR + tt:cNVAR + tt + 1],
        r.STATS[:, cMU + tt:cMU + tt + 1],
        r.STATS[:, cMU + tt:cMU + tt + 1],
        r.STATS[:, cNVAR + tt:cNVAR + tt + 1], OP.mult, OP.subtract))(tt),
        waits=[gmu, gnv], inc="V")
    gst = kb.op("scalar", (lambda tt: lambda e: e.activation(
        r.STATS[:, cSTD + tt:cSTD + tt + 1],
        r.STATS[:, cNVAR + tt:cNVAR + tt + 1], AF.Sqrt,
        bias=EPS, scale=-1.0))(tt), waits=[gnv], inc="S")
    grs = kb.op("vector", (lambda tt: lambda e: e.reciprocal(
        r.STATS[:, cRSTD + tt:cRSTD + tt + 1],
        r.STATS[:, cSTD + tt:cSTD + tt + 1]))(tt), waits=[gst], inc="V")
    gnm = kb.op("vector", (lambda tt: lambda e: e.tensor_scalar(
        r.STATS[:, cNMRS + tt:cNMRS + tt + 1],
        r.STATS[:, cMU + tt:cMU + tt + 1],
        r.STATS[:, cRSTD + tt:cRSTD + tt + 1], -1.0, OP.mult, OP.mult))(tt),
        waits=[grs], inc="V")
    gn = kb.op("vector", (lambda tt: lambda e: e.tensor_scalar(
        XT[:, tt * DM:(tt + 1) * DM], XT[:, tt * DM:(tt + 1) * DM],
        r.STATS[:, cRSTD + tt:cRSTD + tt + 1],
        r.STATS[:, cNMRS + tt:cNMRS + tt + 1],
        OP.mult, OP.add))(tt),
        waits=[gnm], inc="V")
    kb.join_ln_gate = gn


def emit_f(nc, kb, d, r, c):
    es = ExitStack()
    sb = lambda n, sh, dt: es.enter_context(nc.sbuf_tensor(n, sh, dt))
    XNFM = sb(f"XNFM{c}", [128, NMK * T], BF16)
    HF = sb(f"HF{c}", [128, NDC * T], BF16)
    OUT0 = sb(f"OUT0{c}", [128, NTT * DM], BF16)
    XOUT = sb(f"XOUT{c}", [128, NTT * DM], F32)
    FTMP = sb(f"FTMP{c}", [128, 2 * 512], F32)
    t0 = c * T

    kb.begin_phase()

    PA = [es.enter_context(nc.psum_tensor(f"PAA{i}_{c}", [128, 512], F32))
          for i in range(4)]
    PF = [es.enter_context(nc.psum_tensor(f"PFF{i}_{c}", [128, 512], F32))
          for i in range(4)]

    PTB = [PF[0][:].bitcast(BF16), PF[1][:].bitcast(BF16)]
    xnfm_gate = [None] * NMK
    bank_evac_gate = [None, None]
    bank_fill_gate = [None, None]
    for b in range(8):
        bk = b % 2
        for j in range(8):
            i = b * 8 + j
            mk, tt = i // 4, i % 4
            w = []
            if j == 0 and bank_evac_gate[bk] is not None:
                w.append(bank_evac_gate[bk])
            gt = kb.op("tensor", (lambda bk, j, mk, tt: lambda e: e.transpose(
                PTB[bk][:, j * 128:(j + 1) * 128],
                r.XNTMB[:, tt * DM + mk * 128:tt * DM + (mk + 1) * 128],
                r.IDT[:]))(bk, j, mk, tt), waits=w, inc="T")
        bank_fill_gate[bk] = gt
        ge = None
        for j in range(8):
            i = b * 8 + j
            mk, tt = i // 4, i % 4
            ge = kb.op("scalar", (lambda bk, j, mk, tt: lambda e: e.copy(
                XNFM[:, mk * T + tt * 128:mk * T + (tt + 1) * 128],
                PTB[bk][:, j * 128:(j + 1) * 128]))(bk, j, mk, tt),
                waits=[bank_fill_gate[bk]], inc="S")
            if i % 4 == 3:
                xnfm_gate[mk] = ge
        bank_evac_gate[bk] = ge

    def aside(wsrc, wckey, bias_col, func, bank_free, extra_wait=None):
        h_gate = [None] * NDC
        for g in range(8):
            gm = None
            for half in range(2):
                wc = [kb.cast_done[wckey]]
                sl, gw = kb.wload(wsrc.ap(), half * 1024, g * 512, waits=wc)
                for jj in range(8):
                    mk = half * 8 + jj
                    for fcw in range(4):
                        w = [gw, xnfm_gate[mk]]
                        if mk == 0 and bank_free[fcw] is not None:
                            w = w + [bank_free[fcw]]
                        gm = kb.op("tensor", (lambda sl, jj, fcw, mk: lambda e:
                                   e.matmul(
                                       PA[fcw][:],
                                       r.WSTR[:, sl * 4096 + jj * 512 + fcw * 128:
                                              sl * 4096 + jj * 512 + (fcw + 1) * 128],
                                       XNFM[:, mk * T:(mk + 1) * T],
                                       start=(mk == 0), stop=(mk == NMK - 1)))(
                                           sl, jj, fcw, mk),
                            waits=w, inc="T")
                kb.slot_rel[sl] = gm
            for fcw in range(4):
                fc = g * 4 + fcw
                w = [gm]
                if extra_wait is not None and extra_wait[fc] is not None:
                    w.append(extra_wait[fc])
                gh = kb.op("scalar", (lambda fc, fcw: lambda e: e.activation(
                    HF[:, fc * T:(fc + 1) * T], PA[fcw][:], func,
                    bias=bias_col[:, fc:fc + 1]))(fc, fcw), waits=w, inc="S")
                h_gate[fc] = gh
                bank_free[fcw] = gh
        return h_gate

    def bside(wsrc, wckey, h_gate, evac_fn, pf_free):
        hf_last = [None] * NDC
        for mm_ in range(4):
            gm = None
            for kh in range(4):
                wc = [kb.cast_done[wckey]]
                sl, gw = kb.wload(wsrc.ap(), kh * 1024, mm_ * 512, waits=wc)
                for jj in range(8):
                    k = kh * 8 + jj
                    for tt in range(NTT):
                        w = [gw, h_gate[k]]
                        if k == 0 and pf_free[tt] is not None:
                            w = w + [pf_free[tt]]
                        gm = kb.op("tensor", (lambda sl, jj, k, tt: lambda e:
                                   e.matmul(
                                       PF[tt][:],
                                       HF[:, k * T + tt * 128:k * T + (tt + 1) * 128],
                                       r.WSTR[:, sl * 4096 + jj * 512:
                                              sl * 4096 + (jj + 1) * 512],
                                       start=(k == 0), stop=(k == NDC - 1)))(
                                           sl, jj, k, tt),
                            waits=w, inc="T")
                    if mm_ == 3:
                        hf_last[k] = gm
                kb.slot_rel[sl] = gm
            for tt in range(NTT):
                pf_free[tt] = evac_fn(mm_, tt, gm)
        return hf_last

    bank_free = [None] * 4
    h_gate = aside(d.w0ab, "w0a", r.B0AC, AF.Gelu, bank_free)
    pf_free = [None] * 4

    def evac0(mm_, tt, gate):
        return kb.op("vector", (lambda mm_, tt: lambda e: e.tensor_add(
            OUT0[:, tt * DM + mm_ * 512:tt * DM + (mm_ + 1) * 512], PF[tt][:],
            r.B0BBC[:, mm_ * 512:(mm_ + 1) * 512]))(mm_, tt),
            waits=[gate], inc="V")
    hf_last = bside(d.w0bb, "w0b", h_gate, evac0, pf_free)

    h_gate1 = aside(d.w1ab, "w1a", r.B1AC, AF.Relu, bank_free,
                    extra_wait=hf_last)

    ln2_v = [None] * NTT

    def evac1(mm_, tt, gate):
        g1 = kb.op("vector", (lambda mm_, tt: lambda e: e.tensor_add(
            FTMP[:, 0:512], PF[tt][:],
            r.B1BBC[:, mm_ * 512:(mm_ + 1) * 512]))(mm_, tt),
            waits=[gate], inc="V")
        g2 = kb.op("vector", (lambda mm_, tt: lambda e: e.tensor_sub(
            FTMP[:, 512:1024],
            OUT0[:, tt * DM + mm_ * 512:tt * DM + (mm_ + 1) * 512],
            FTMP[:, 0:512]))(mm_, tt), waits=[g1], inc="V")
        g3 = kb.op("vector", (lambda mm_, tt: lambda e: e.scalar_tensor_tensor(
            FTMP[:, 0:512], FTMP[:, 512:1024],
            r.STATS[:, CMASK + tt:CMASK + tt + 1], FTMP[:, 0:512],
            OP.mult, OP.add))(mm_, tt), waits=[g2], inc="V")
        g4 = kb.op("vector", (lambda mm_, tt: lambda e: e.scalar_tensor_tensor(
            XOUT[:, tt * DM + mm_ * 512:tt * DM + (mm_ + 1) * 512],
            FTMP[:, 0:512], 1.0,
            r.XNTMB[:, tt * DM + mm_ * 512:tt * DM + (mm_ + 1) * 512],
            OP.mult, OP.add,
            accum_out=r.STATS[:, S2SUM + tt * 4 + mm_:
                              S2SUM + tt * 4 + mm_ + 1]))(mm_, tt),
            waits=[g3], inc="V")
        ln2_v[tt] = g4
        return g4
    bside(d.w1bb, "w1b", h_gate1, evac1, pf_free)

    for tt in range(NTT):
        emit_ln(nc, kb, r, XOUT, tt, ln2_v[tt],
                S2SUM, S2SSQ, C2MU, C2NVAR, C2STD, C2RSTD, C2NMRS, bf16=False)
        gn2 = kb.join_ln_gate
        kb.dma("sync", d.out.ap()[t0 + tt * 128:t0 + (tt + 1) * 128, :],
               XOUT[:, tt * DM:(tt + 1) * DM], waits=[gn2])

    kb.end_phase(f"f_{c}")
    es.close()


def build():
    nc = bass.Bass("TRN2", target_bir_lowering=False)
    d = declare_dram(nc)
    with ExitStack() as es:
        kb = KB(nc, es)
        r = alloc_resident(nc, es)
        kb.WSTR = r.WSTR
        nc.const_aps.register(F32, EPS, r.CONSTS[:, 0:1])
        nc.const_aps.register(F32, -1.0, r.CONSTS[:, 1:2])
        nc.const_aps.register(F32, 1.0, r.CONSTS[:, 2:3])
        for c in range(NCH):
            with ExitStack() as mes:
                m = NS()
                sb = lambda n, sh, dt: mes.enter_context(nc.sbuf_tensor(n, sh, dt))
                m.U = sb(f"U{c}", [128, NDC * T], BF16)
                m.SILUZ = sb(f"SILUZ{c}", [128, NDC * T], BF16)
                m.DTRB = sb(f"DTRB{c}", [128, T], BF16)
                m.BCR = sb(f"BCR{c}", [32, T], BF16)
                if c == 0:
                    with nc.sbuf_tensor("XBF0", [128, NMK * T], BF16) as XBF0:
                        emit_prologue(nc, kb, d, r, XBF0)
                        emit_m1(nc, kb, d, r, m, c, XBF0=XBF0)
                else:
                    emit_m1(nc, kb, d, r, m, c)
                emit_m2(nc, kb, d, r, m, c)
            emit_f(nc, kb, d, r, c)
    return nc


_NC_CACHE = None


def kernel(**inputs):
    global _NC_CACHE
    if _NC_CACHE is None:
        _NC_CACHE = build()
    nc = _NC_CACHE

    f32 = np.float32
    asf = lambda a: np.asarray(a, dtype=f32)
    w = {
        "winT": asf(inputs["L0_attn_W_in"]).T,
        "convw": asf(inputs["L0_attn_conv_w"]),
        "convb": asf(inputs["L0_attn_conv_b"]),
        "wxT": asf(inputs["L0_attn_W_x"]).T,
        "wdtT": asf(inputs["L0_attn_W_dt"]).T,
        "bdt": asf(inputs["L0_attn_b_dt"]),
        "alog": asf(inputs["L0_attn_A_log"]),
        "dd": asf(inputs["L0_attn_D"]),
        "woutT": asf(inputs["L0_attn_W_out"]).T,
        "wg": asf(inputs["L0_ffn_Wg"]),
        "w0aT": asf(inputs["L0_ffn_W0a"]).T,
        "b0a": asf(inputs["L0_ffn_b0a"]),
        "w0bT": asf(inputs["L0_ffn_W0b"]).T,
        "b0b": asf(inputs["L0_ffn_b0b"]),
        "w1aT": asf(inputs["L0_ffn_W1a"]).T,
        "b1a": asf(inputs["L0_ffn_b1a"]),
        "w1bT": asf(inputs["L0_ffn_W1b"]).T,
        "b1b": asf(inputs["L0_ffn_b1b"]),
    }
    w = {k: np.ascontiguousarray(v) for k, v in w.items()}
    w["cid"] = np.eye(128, dtype=ml_dtypes.bfloat16)
    w["cones"] = np.ones((1, 128), dtype=f32)
    w["cid32f"] = np.eye(32, dtype=f32)
    esel = np.zeros((32, 32 * 128), dtype=ml_dtypes.bfloat16)
    for s in range(32):
        esel[s, s * 128:(s + 1) * 128] = 1
    w["esel"] = esel

    x = np.asarray(inputs["L0_attn_x"], dtype=f32)
    in_maps = []
    for i in range(8):
        m = dict(w)
        m["x"] = np.ascontiguousarray(x[i])
        m["xT"] = np.ascontiguousarray(x[i].T)
        in_maps.append(m)

    res = run_bass_kernel_spmd(nc, in_maps, core_ids=list(range(8)))
    return np.stack([np.asarray(r["out"]) for r in res.results]).astype(np.float32)
